# version 05
# baseline (speedup 1.0000x reference)
import numpy as np
import ml_dtypes
from contextlib import ExitStack

import concourse.bass as bass
import concourse.mybir as mybir
from concourse.bass_utils import run_bass_kernel_spmd

F32 = mybir.dt.float32
BF16 = mybir.dt.bfloat16
I32 = mybir.dt.int32
AF = mybir.ActivationFunctionType
ALU = mybir.AluOpType
AX = mybir.AxisListType
ET = mybir.EngineType

D, DH, NE, T = 1024, 4096, 8, 4096
TG = 256
NTG = T // TG
TT = TG // 128
HC, DC = DH // 128, D // 128
L = 8192
FG = 4
NSEQ = 512
NFG = NSEQ // FG

DEBUG_STAGE = 0

MC_SUM, MC_SQ, MC_MEAN, MC_NEG, MC_Q, MC_VE, MC_LNV, MC_RSTD = 0, 4, 8, 12, 16, 20, 24, 28
MC_LG, MC_MX, MC_NMX, MC_EXP, MC_SUME, MC_WINV, MC_MSK = 32, 60, 62, 64, 80, 82, 88



def _tables():
    t = {}
    kk = np.arange(128)
    a1 = 2 * np.pi * (np.arange(64)[:, None] * kk[None, :]) / 128.0
    t["ww1"] = np.concatenate([np.cos(a1), -np.sin(a1)], 1).astype(np.float32)
    aT = 2 * np.pi * (np.arange(128)[:, None] * kk[None, :]) / 16384.0
    t["twr"] = np.tile(np.cos(aT), (1, FG)).astype(np.float32)
    t["twi"] = np.tile(-np.sin(aT), (1, FG)).astype(np.float32)
    t["twri"] = np.tile(np.cos(aT) / 16384.0, (1, FG)).astype(np.float32)
    t["twii"] = np.tile(np.sin(aT) / 16384.0, (1, FG)).astype(np.float32)
    a2 = 2 * np.pi * (np.arange(128)[:, None] * kk[None, :]) / 128.0
    COS, SIN = np.cos(a2), np.sin(a2)
    t["coss"], t["sins"], t["nsins"] = (COS.astype(np.float32),
                                        SIN.astype(np.float32), (-SIN).astype(np.float32))
    t["wwA"] = np.concatenate([COS, SIN], 1).astype(np.float32)
    t["wwB"] = np.concatenate([-SIN, COS], 1).astype(np.float32)
    aB = 2 * np.pi * (np.arange(128)[:, None] * np.arange(64)[None, :]) / 128.0
    t["cosb"] = np.cos(aB).astype(np.float32)
    t["nsinb"] = (-np.sin(aB)).astype(np.float32)
    t["ident"] = np.eye(128, dtype=np.float32)
    return t


def _prep(inputs):
    f32 = np.float32
    W1 = np.ascontiguousarray(np.asarray(inputs["L0_ffn_W1"], f32))
    W2 = np.ascontiguousarray(np.asarray(inputs["L0_ffn_W2"], f32))
    w1t = np.ascontiguousarray(W1.reshape(NE, DC, 128, HC, 128).transpose(0, 2, 3, 1, 4))
    b1 = np.asarray(inputs["L0_ffn_b1"], f32)
    b1g = np.ascontiguousarray(b1.reshape(NE, HC, 128).transpose(2, 0, 1).reshape(128, NE * HC))
    b2r = np.ascontiguousarray(np.asarray(inputs["L0_ffn_b2"], f32).reshape(1, NE * D))
    wg = np.ascontiguousarray(np.asarray(inputs["L0_ffn_Wg"], f32))
    bg = np.ascontiguousarray(np.asarray(inputs["L0_ffn_bg"], f32).reshape(1, NE))

    ipw = np.asarray(inputs["L0_attn_in_proj_w"], f32)
    ipb = np.asarray(inputs["L0_attn_in_proj_b"], f32)
    Wv, bv = ipw[:, 0::3], ipb[0::3]
    Wx2, bx2 = ipw[:, 2::3], ipb[2::3]

    def proj_tiles(W):
        tl = W.reshape(DC, 128, DC, 128).transpose(1, 2, 0, 3)
        return np.ascontiguousarray(tl.reshape(128, DC * DC * 128))

    shared = dict(
        w1t=w1t, w2r=W2, b1g=b1g, b2r=b2r, wg=wg, bg=bg,
        wvt=proj_tiles(Wv), wx2t=proj_tiles(Wx2),
        bvr=np.ascontiguousarray(bv.reshape(1, D)),
        bx2r=np.ascontiguousarray(bx2.reshape(1, D)),
        owr=np.ascontiguousarray(np.asarray(inputs["L0_attn_out_w"], f32)),
        obr=np.ascontiguousarray(np.asarray(inputs["L0_attn_out_b"], f32).reshape(1, D)),
        fw1=np.ascontiguousarray(np.asarray(inputs["L0_attn_fw1"], f32)),
        fb1g=np.ascontiguousarray(np.asarray(inputs["L0_attn_fb1"], f32).reshape(64, 1)),
        fw2=np.ascontiguousarray(np.asarray(inputs["L0_attn_fw2"], f32)),
        fb2g=np.ascontiguousarray(np.asarray(inputs["L0_attn_fb2"], f32).reshape(128, 1)),
    )
    shared.update(_tables())

    fw3 = np.asarray(inputs["L0_attn_fw3"], f32)
    fb3 = np.asarray(inputs["L0_attn_fb3"], f32)
    fw3o, fb3o = fw3[:, 1::2], fb3[1::2]
    sw = np.asarray(inputs["L0_attn_short_w"], f32)
    sbv = np.asarray(inputs["L0_attn_short_b"], f32)
    x = np.asarray(inputs["L0_attn_x"], f32)
    per_core = []
    for b in range(4):
        for h in range(2):
            c0 = 512 * h
            per_core.append(dict(
                x_sh=np.ascontiguousarray(x[b, 4096 * h:4096 * h + 4096, :]),
                fw3l=np.ascontiguousarray(fw3o[:, c0:c0 + 512]),
                fb3r=np.ascontiguousarray(fb3o[c0:c0 + 512].reshape(1, 512)),
                swl=np.ascontiguousarray(
                    sw[c0:c0 + 512, 0, :].reshape(4, 128, 3).transpose(1, 0, 2).reshape(128, 12)),
                sbl=np.ascontiguousarray(sbv[c0:c0 + 512].reshape(4, 128).T),
            ))
    return shared, per_core



class BB:
    pass


def S(bb, n):
    return bb.sems[n]


def _alloc(bb, nc):
    dram = lambda n, s, k="ExternalInput": nc.dram_tensor(n, s, F32, kind=k).ap()
    for n, s in [("x_sh", [T, D]), ("w1t", [NE, 128, HC, DC, 128]), ("w2r", [NE, DH, D]),
                 ("b1g", [128, NE * HC]), ("b2r", [1, NE * D]), ("wg", [D, NE]), ("bg", [1, NE]),
                 ("wvt", [128, DC * DC * 128]), ("wx2t", [128, DC * DC * 128]),
                 ("bvr", [1, D]), ("bx2r", [1, D]), ("owr", [D, D]), ("obr", [1, D]),
                 ("fw1", [1, 64]), ("fb1g", [64, 1]), ("fw2", [64, 128]), ("fb2g", [128, 1]),
                 ("fw3l", [128, 512]), ("fb3r", [1, 512]), ("swl", [128, 12]), ("sbl", [128, 4]),
                 ("ww1", [64, 256]), ("twr", [128, 512]), ("twi", [128, 512]),
                 ("twri", [128, 512]), ("twii", [128, 512]),
                 ("coss", [128, 128]), ("sins", [128, 128]), ("nsins", [128, 128]),
                 ("wwA", [128, 256]), ("wwB", [128, 256]),
                 ("cosb", [128, 64]), ("nsinb", [128, 64]), ("ident", [128, 128])]:
        setattr(bb, n, dram(n, s))
    bb.y_sh = dram("y_sh", [T, D], k="ExternalOutput")

    idram = lambda n, s, d: nc.dram_tensor(n, s, d, kind="Internal").ap()
    bb.u_scr = idram("u_scr", [D, T], BF16)
    bb.y_scr = idram("y_scr", [T, D], F32)
    bb.x1_scr = idram("x1_scr", [T, D], F32)
    bb.xb2_scr = idram("xb2_scr", [D, T], BF16)
    bb.vin_cc = idram("vin_cc", [D, T], BF16)
    bb.vout_cc = idram("vout_cc", [2 * D, T], BF16)
    bb.vpr_scr = idram("vpr_scr", [512, L], BF16)
    bb.filt_scr = idram("filt_scr", [512, L], BF16)
    bb.hspec_scr = idram("hspec_scr", [512, 256 * 128], BF16)
    bb.cin_cc = idram("cin_cc", [512 * 64, 128], BF16)
    bb.cout_cc = idram("cout_cc", [2 * 512 * 64, 128], BF16)

    sb = lambda n, s, d: bb.ctx.enter_context(nc.sbuf_tensor(n, s, d))
    bb.w1sb = sb("w1sb", [128, HC * DC * 128], BF16)
    bb.w2sb = sb("w2sb", [128, HC * D], BF16)
    bb.hu_sb = sb("hu_sb", [128, 4096], BF16)
    bb.uT_sb = sb("uT_sb", [128, 2 * DC * TG], BF16)
    bb.x_sb = sb("x_sb", [128, TT * D], F32)
    bb.tmp_sb = sb("tmp_sb", [128, 2 * TT * D], F32)
    bb.s_sb = sb("s_sb", [128, (T // 128) * NE], F32)
    bb.misc = sb("misc", [128, 128], F32)
    bb.b1_sb = sb("b1_sb", [128, NE * HC], F32)
    bb.b2e_sb = sb("b2e_sb", [1, D], BF16)
    bb.bg_sb = sb("bg_sb", [1, NE], BF16)
    bb.bvr_sb = sb("bvr_sb", [1, D], BF16)
    bb.bx2r_sb = sb("bx2r_sb", [1, D], BF16)
    bb.obr_sb = sb("obr_sb", [1, D], BF16)
    bb.wg_sb = sb("wg_sb", [128, DC * NE], BF16)
    bb.ones_sb = sb("ones_sb", [1, 512], BF16)
    bb.id_sb = sb("id_sb", [128, 128], BF16)
    bb.ww1_sb = sb("ww1_sb", [64, 256], BF16)
    bb.coss_sb = sb("coss_sb", [128, 128], BF16)
    bb.sins_sb = sb("sins_sb", [128, 128], BF16)
    bb.nsins_sb = sb("nsins_sb", [128, 128], BF16)
    bb.wwA_sb = sb("wwA_sb", [128, 256], BF16)
    bb.wwB_sb = sb("wwB_sb", [128, 256], BF16)
    bb.cosb_sb = sb("cosb_sb", [128, 64], BF16)
    bb.nsinb_sb = sb("nsinb_sb", [128, 64], BF16)
    bb.twr_sb = sb("twr_sb", [128, 512], F32)
    bb.twi_sb = sb("twi_sb", [128, 512], F32)
    bb.twri_sb = sb("twri_sb", [128, 512], F32)
    bb.twii_sb = sb("twii_sb", [128, 512], F32)
    bb.fw1_sb = sb("fw1_sb", [1, 64], F32)
    bb.fb1_sb = sb("fb1_sb", [64, 1], F32)
    bb.fw2_sb = sb("fw2_sb", [64, 128], BF16)
    bb.fb2_sb = sb("fb2_sb", [128, 1], F32)
    bb.fw3_sb = sb("fw3_sb", [128, 512], BF16)
    bb.fb3_sb = sb("fb3_sb", [1, 512], BF16)
    bb.swl_sb = sb("swl_sb", [128, 12], F32)
    bb.sbl_sb = sb("sbl_sb", [128, 4], F32)
    bb.io_sb = sb("io_sb", [128, 512], I32)
    bb.pos_sb = sb("pos_sb", [1, 1024], F32)

    bb.fA = lambda b: bb.w1sb[0:64, b * 512:(b + 1) * 512]
    bb.fC = lambda b: bb.w1sb[:, 1024 + b * 1024: 1024 + (b + 1) * 1024]
    bb.fS = lambda b: bb.w1sb[:, 3072 + b * 1024: 3072 + (b + 1) * 1024]
    bb.fP = lambda b: bb.w1sb[:, 5120 + b * 1024: 5120 + (b + 1) * 1024]
    bb.fG = lambda b: bb.w1sb[:, 7168 + b * 1024: 7168 + (b + 1) * 1024]
    bb.fy = lambda b: bb.w1sb[0:64, 9216 + b * 512: 9216 + (b + 1) * 512]
    bb.fT = lambda b: bb.w1sb[:, 10240 + b * 1024: 10240 + (b + 1) * 1024]
    bb.fH = lambda b: bb.hu_sb[:, b * 1024:(b + 1) * 1024]
    bb.vrow = bb.w2sb[:, 0:L + 2]
    bb.vrow2 = bb.w2sb[:, 8256:8256 + L]
    bb.h1_sb = bb.w2sb[0:64, 0:L]
    bb.h2_sb = bb.w2sb[:, 8192:16384]

    ps = lambda n, s, d: bb.ctx.enter_context(nc.psum_tensor(n, s, d))
    bb.psA = ps("psA", [128, 256], F32)
    bb.psB = ps("psB", [128, 256], F32)
    bb.psC = ps("psC", [128, 1024], F32)
    bb.psD = ps("psD", [128, 1024], F32)
    bb.psT1 = ps("psT1", [128, 512], BF16)
    bb.psT2 = ps("psT2", [128, 512], BF16)

    bb.sems = {}
    bb.cnt = {}
    for name in ["c", "x", "u", "ul", "w", "acc", "st", "cc",
                 "peh", "peo", "pet", "se", "ve", "po",
                 "fa", "fh", "fpe", "fve", "fpo", "fst"]:
        bb.sems[name] = bb.ctx.enter_context(nc.semaphore("sem_" + name))
        bb.cnt[name] = 0



def emit_p0(bb, nc):
    casts = [
        (bb.wg_sb.rearrange("p (dc e) -> p dc e", dc=DC), bb.wg.rearrange("(dc p) e -> p dc e", p=128)),
        (bb.ww1_sb[:, :], bb.ww1), (bb.coss_sb[:, :], bb.coss),
        (bb.sins_sb[:, :], bb.sins), (bb.nsins_sb[:, :], bb.nsins),
        (bb.wwA_sb[:, :], bb.wwA), (bb.wwB_sb[:, :], bb.wwB),
        (bb.cosb_sb[:, :], bb.cosb), (bb.nsinb_sb[:, :], bb.nsinb),
        (bb.id_sb[:, :], bb.ident),
        (bb.fw2_sb[:, :], bb.fw2), (bb.fw3_sb[:, :], bb.fw3l),
    ]
    casts += [(bb.bg_sb[:, :], bb.bg), (bb.obr_sb[:, :], bb.obr),
              (bb.bvr_sb[:, :], bb.bvr), (bb.bx2r_sb[:, :], bb.bx2r),
              (bb.fb3_sb[:, :], bb.fb3r)]
    plains = [
        (bb.b1_sb[:, :], bb.b1g),
        (bb.fw1_sb[:, :], bb.fw1),
        (bb.fb1_sb[:, :], bb.fb1g), (bb.fb2_sb[:, :], bb.fb2g),
        (bb.swl_sb[:, :], bb.swl), (bb.sbl_sb[:, :], bb.sbl),
        (bb.twr_sb[:, :], bb.twr), (bb.twi_sb[:, :], bb.twi),
        (bb.twri_sb[:, :], bb.twri), (bb.twii_sb[:, :], bb.twii),
    ]
    n = len(casts) + len(plains)
    with nc.Block() as blk:
        @blk.gpsimd
        def _(g):
            for dst, src in casts:
                g.dma_start(out=dst, in_=src).then_inc(S(bb, "c"), 16)
            for dst, src in plains:
                g.dma_start(out=dst, in_=src).then_inc(S(bb, "c"), 16)
            g.iota(bb.io_sb[:, :], [[1, 512]], channel_multiplier=0,
                   allow_small_or_imprecise_dtypes=True)
            g.wait_ge(S(bb, "c"), 16 * n)

        @blk.vector
        def _(v):
            v.memset(bb.ones_sb[:, :], 1.0)
    bb.cnt["c"] += 16 * n


def emit_lnpass(bb, nc, src, gating):
    VE = 8 if gating else 6
    SE = 4 if gating else 3
    PE = 5 if gating else 4
    bx, bu = bb.cnt["x"], bb.cnt["u"]
    bve, bse, bpe = bb.cnt["ve"], bb.cnt["se"], bb.cnt["pet"]
    u_sb = bb.hu_sb[:, 0:TT * D]

    with nc.Block() as blk:
        @blk.sync
        def _(sy):
            for g in range(NTG):
                if g >= 1:
                    sy.wait_ge(S(bb, "ve"), bve + (g - 1) * VE + 2)
                sy.dma_start(
                    out=bb.x_sb.rearrange("p (tt d) -> p tt d", tt=TT),
                    in_=src[g * TG:(g + 1) * TG, :].rearrange("(tt p) d -> p tt d", p=128),
                ).then_inc(S(bb, "x"), 16)
                sy.wait_ge(S(bb, "ve"), bve + g * VE + 6)
                for dc in range(DC):
                    sy.dma_start(
                        out=bb.u_scr[dc * 128:(dc + 1) * 128, g * TG:(g + 1) * TG],
                        in_=bb.uT_sb[:, dc * TG:(dc + 1) * TG],
                    ).then_inc(S(bb, "u"), 16)

        @blk.scalar
        def _(sc):
            for g in range(NTG):
                sc.wait_ge(S(bb, "x"), bx + (g + 1) * 16)
                if g >= 1:
                    sc.wait_ge(S(bb, "pet"), bpe + (g - 1) * PE + 4)
                sc.activation(u_sb[:, 0:D], bb.x_sb[:, 0:D], AF.Square,
                              accum_out=bb.misc[:, MC_SQ:MC_SQ + 1])
                sc.activation(u_sb[:, D:2 * D], bb.x_sb[:, D:2 * D], AF.Square,
                              accum_out=bb.misc[:, MC_SQ + 1:MC_SQ + 2])
                sc.activation(bb.misc[:, 124:125], bb.misc[:, 125:126], AF.Copy)
                sc.activation(bb.misc[:, 124:125], bb.misc[:, 125:126],
                              AF.Copy).then_inc(S(bb, "se"), 1)
                sc.wait_ge(S(bb, "ve"), bve + g * VE + 1)
                sc.activation(bb.misc[:, MC_LNV:MC_LNV + TT],
                              bb.misc[:, MC_VE:MC_VE + TT], AF.Ln).then_inc(S(bb, "se"), 1)
                sc.activation(bb.misc[:, 126:127], bb.misc[:, 127:128], AF.Copy)
                sc.activation(bb.misc[:, MC_RSTD:MC_RSTD + TT],
                              bb.misc[:, MC_LNV:MC_LNV + TT], AF.Exp,
                              scale=-0.5).then_inc(S(bb, "se"), 1)
                if gating:
                    sc.wait_ge(S(bb, "ve"), bve + g * VE + 7)
                    sc.activation(bb.misc[:, MC_EXP:MC_EXP + 8],
                                  bb.psA[:, 0:8], AF.Exp,
                                  bias=bb.misc[:, MC_NMX:MC_NMX + 1])
                    sc.activation(bb.misc[:, MC_EXP + 8:MC_EXP + 16],
                                  bb.psA[:, 8:16], AF.Exp,
                                  bias=bb.misc[:, MC_NMX + 1:MC_NMX + 2]
                                  ).then_inc(S(bb, "se"), 1)

        @blk.vector
        def _(v):
            for g in range(NTG):
                v.wait_ge(S(bb, "x"), bx + (g + 1) * 16)
                for tt in range(TT):
                    v.reduce_sum(bb.misc[:, MC_SUM + tt:MC_SUM + tt + 1],
                                 bb.x_sb[:, tt * D:(tt + 1) * D], axis=AX.X)
                v.tensor_copy(bb.misc[:, 126:127], bb.misc[:, 127:128])
                v.tensor_scalar(bb.misc[:, MC_MEAN:MC_MEAN + TT],
                                bb.misc[:, MC_SUM:MC_SUM + TT],
                                float(1.0 / D), None, ALU.mult)
                v.tensor_scalar(bb.misc[:, MC_NEG:MC_NEG + TT],
                                bb.misc[:, MC_SUM:MC_SUM + TT],
                                float(-1.0 / D), None, ALU.mult)
                v.wait_ge(S(bb, "se"), bse + g * SE + 1)
                v.tensor_scalar(bb.misc[:, MC_Q:MC_Q + TT],
                                bb.misc[:, MC_SQ:MC_SQ + TT],
                                float(1.0 / D), float(1e-5), ALU.mult, ALU.add)
                v.tensor_copy(bb.misc[:, 126:127], bb.misc[:, 127:128])
                for tt in range(TT):
                    v.scalar_tensor_tensor(
                        bb.misc[:, MC_VE + tt:MC_VE + tt + 1],
                        bb.misc[:, MC_MEAN + tt:MC_MEAN + tt + 1],
                        bb.misc[:, MC_NEG + tt:MC_NEG + tt + 1],
                        bb.misc[:, MC_Q + tt:MC_Q + tt + 1], ALU.mult, ALU.add)
                v.tensor_copy(bb.misc[:, 126:127], bb.misc[:, 127:128])
                v.tensor_copy(bb.misc[:, 126:127],
                              bb.misc[:, 127:128]).then_inc(S(bb, "ve"), 1)
                v.wait_ge(S(bb, "se"), bse + g * SE + 3)
                for tt in range(TT):
                    v.tensor_scalar(u_sb[:, tt * D:(tt + 1) * D],
                                    bb.x_sb[:, tt * D:(tt + 1) * D],
                                    bb.misc[:, MC_NEG + tt:MC_NEG + tt + 1],
                                    bb.misc[:, MC_RSTD + tt:MC_RSTD + tt + 1],
                                    ALU.add, ALU.mult)
                v.tensor_copy(bb.misc[:, 126:127], bb.misc[:, 127:128])
                v.tensor_copy(bb.misc[:, 126:127],
                              bb.misc[:, 127:128]).then_inc(S(bb, "ve"), 1)
                for q in range(4):
                    v.wait_ge(S(bb, "pet"), bpe + g * PE + q + 1)
                    pst = bb.psT1 if q % 2 == 0 else bb.psT2
                    for j in range(4):
                        idx = q * 4 + j
                        tt2, dc = divmod(idx, DC)
                        v.tensor_copy(
                            bb.uT_sb[:, dc * TG + tt2 * 128: dc * TG + (tt2 + 1) * 128],
                            pst[:, j * 128:(j + 1) * 128])
                    v.tensor_copy(bb.misc[:, 126:127], bb.misc[:, 127:128])
                    v.tensor_copy(bb.misc[:, 126:127],
                                  bb.misc[:, 127:128]).then_inc(S(bb, "ve"), 1)
                if gating:
                    v.wait_ge(S(bb, "pet"), bpe + g * PE + 5)
                    for tt in range(TT):
                        v.reduce_max(bb.misc[:, MC_MX + tt:MC_MX + tt + 1],
                                     bb.psA[:, tt * 8:(tt + 1) * 8], axis=AX.X)
                        v.reduce_max(bb.misc[:, MC_NMX + tt:MC_NMX + tt + 1],
                                     bb.psA[:, tt * 8:(tt + 1) * 8], axis=AX.X, negate=True)
                    ins = None
                    for tt in range(TT):
                        ins = v.tensor_scalar(bb.misc[:, MC_MSK + 8 * tt:MC_MSK + 8 * tt + 8],
                                              bb.psA[:, tt * 8:(tt + 1) * 8],
                                              bb.misc[:, MC_MX + tt:MC_MX + tt + 1],
                                              None, ALU.is_equal)
                    ins.then_inc(S(bb, "ve"), 1)
                    v.wait_ge(S(bb, "se"), bse + g * SE + 4)
                    for tt in range(TT):
                        v.reduce_sum(bb.misc[:, MC_SUME + tt:MC_SUME + tt + 1],
                                     bb.misc[:, MC_EXP + 8 * tt:MC_EXP + 8 * tt + 8], axis=AX.X)
                    for tt in range(TT):
                        v.reciprocal(bb.misc[:, MC_WINV + tt:MC_WINV + tt + 1],
                                     bb.misc[:, MC_SUME + tt:MC_SUME + tt + 1])
                    for tt in range(TT):
                        col = (g * TT + tt) * NE
                        v.tensor_scalar(bb.s_sb[:, col:col + NE],
                                        bb.misc[:, MC_MSK + 8 * tt:MC_MSK + 8 * tt + 8],
                                        bb.misc[:, MC_WINV + tt:MC_WINV + tt + 1],
                                        float(0.5), ALU.mult, ALU.mult)
                    v.tensor_copy(bb.misc[:, 126:127], bb.misc[:, 127:128])
                    v.tensor_copy(bb.misc[:, 126:127],
                                  bb.misc[:, 127:128]).then_inc(S(bb, "ve"), 1)

        @blk.tensor
        def _(t):
            for g in range(NTG):
                t.wait_ge(S(bb, "ve"), bve + g * VE + 2)
                if g >= 1:
                    t.wait_ge(S(bb, "u"), bu + g * 128)
                for q in range(4):
                    if q >= 2:
                        t.wait_ge(S(bb, "ve"), bve + g * VE + q + 1)
                    pst = bb.psT1 if q % 2 == 0 else bb.psT2
                    ins = None
                    for j in range(4):
                        idx = q * 4 + j
                        tt2, dc = divmod(idx, DC)
                        ins = t.transpose(pst[:, j * 128:(j + 1) * 128],
                                          u_sb[:, tt2 * D + dc * 128: tt2 * D + (dc + 1) * 128],
                                          bb.id_sb[:, :])
                    ins.then_inc(S(bb, "pet"), 1)
                if gating:
                    ins = None
                    for tt in range(TT):
                        for dc in range(DC):
                            t.ldweights(bb.uT_sb[:, dc * TG + tt * 128: dc * TG + (tt + 1) * 128])
                            t.matmul(bb.psA[:, tt * 8:(tt + 1) * 8],
                                     bb.uT_sb[:, dc * TG + tt * 128: dc * TG + (tt + 1) * 128],
                                     bb.wg_sb[:, dc * NE:(dc + 1) * NE],
                                     start=(dc == 0), stop=False)
                        t.ldweights(bb.ones_sb[:, 0:128])
                        ins = t.matmul(bb.psA[:, tt * 8:(tt + 1) * 8],
                                       bb.ones_sb[:, 0:128], bb.bg_sb[:, :],
                                       start=False, stop=True)
                    ins.then_inc(S(bb, "pet"), 1)

    bb.cnt["x"] += NTG * 16
    bb.cnt["u"] += NTG * 128
    bb.cnt["ve"] += NTG * VE
    bb.cnt["se"] += NTG * SE
    bb.cnt["pet"] += NTG * PE


def emit_wload(bb, nc, e):
    with nc.Block() as blk:
        @blk.gpsimd
        def _(g):
            g.dma_start(out=bb.w1sb[:, :],
                        in_=bb.w1t[e].rearrange("p hc dc q -> p (hc dc q)")
                        ).then_inc(S(bb, "w"), 16)
            for i in range(4):
                g.dma_start(
                    out=bb.w2sb[:, i * 8 * D:(i + 1) * 8 * D].rearrange("p (hc d) -> p hc d", hc=8),
                    in_=bb.w2r[e, i * 1024:(i + 1) * 1024, :].rearrange(
                        "(hc p) d -> p hc d", p=128),
                ).then_inc(S(bb, "w"), 16)
            g.dma_start(out=bb.b2e_sb[:, :],
                        in_=bb.b2r[:, e * D:(e + 1) * D]).then_inc(S(bb, "w"), 16)
    bb.cnt["w"] += 96


def emit_expert(bb, nc, e):
    bu, bw = bb.cnt["ul"], bb.cnt["w"]
    bh, bo = bb.cnt["peh"], bb.cnt["peo"]
    bs, bv, ba = bb.cnt["se"], bb.cnt["ve"], bb.cnt["acc"]
    u_done = bb.cnt["u"]
    x_done = bb.cnt["x"]
    htile = lambda p: bb.hu_sb[:, p * TG:(p + 1) * TG]

    with nc.Block() as blk:
        @blk.sync
        def _(sy):
            sy.wait_ge(S(bb, "u"), u_done)
            for g in range(NTG):
                if g >= 2:
                    sy.wait_ge(S(bb, "peh"), bh + (g - 1) * HC)
                sy.dma_start(
                    out=bb.uT_sb[:, (g % 2) * DC * TG:((g % 2) + 1) * DC * TG].rearrange("p (dc t) -> p dc t", dc=DC),
                    in_=bb.u_scr[:, g * TG:(g + 1) * TG].rearrange(
                        "(dc p) t -> p dc t", p=128),
                ).then_inc(S(bb, "ul"), 16)

        @blk.tensor
        def _(t):
            t.wait_ge(S(bb, "w"), bw)
            for g in range(NTG):
                t.wait_ge(S(bb, "ul"), bu + (g + 1) * 16)
                ub = (g % 2) * DC * TG
                for hc in range(HC + 1):
                    if hc < HC:
                        lin = g * HC + hc
                        if lin >= 2:
                            t.wait_ge(S(bb, "se"), bs + lin - 1)
                        psh = bb.psA if hc % 2 == 0 else bb.psB
                        ins = None
                        for dc in range(DC):
                            w1tile = bb.w1sb[:, (hc * DC + dc) * 128:(hc * DC + dc + 1) * 128]
                            t.ldweights(w1tile)
                            ins = t.matmul(psh[:, :], w1tile,
                                           bb.uT_sb[:, ub + dc * TG: ub + (dc + 1) * TG],
                                           start=(dc == 0), stop=(dc == DC - 1))
                        ins.then_inc(S(bb, "peh"), 1)
                    if hc >= 1:
                        hcm = hc - 1
                        t.wait_ge(S(bb, "se"), bs + g * HC + hcm + 1)
                        if hcm == 0:
                            if g >= 1:
                                t.wait_ge(S(bb, "ve"), bv + g * 2)
                        for ts in range(TT):
                            pso = bb.psC if ts == 0 else bb.psD
                            ht = htile(hcm % 2)[:, ts * 128:(ts + 1) * 128]
                            t.ldweights(ht)
                            for half in range(2):
                                ins = t.matmul(
                                    pso[:, half * 512:(half + 1) * 512], ht,
                                    bb.w2sb[:, hcm * D + half * 512: hcm * D + (half + 1) * 512],
                                    start=(hcm == 0), stop=False)
                ins = None
                for ts in range(TT):
                    pso = bb.psC if ts == 0 else bb.psD
                    t.ldweights(bb.ones_sb[:, 0:128])
                    for half in range(2):
                        ins = t.matmul(pso[:, half * 512:(half + 1) * 512],
                                       bb.ones_sb[:, 0:128],
                                       bb.b2e_sb[:, half * 512:(half + 1) * 512],
                                       start=False, stop=True)
                    ins.then_inc(S(bb, "peo"), 1)

        @blk.scalar
        def _(sc):
            for g in range(NTG):
                for hc in range(HC):
                    lin = g * HC + hc
                    sc.wait_ge(S(bb, "peh"), bh + lin + 1)
                    psh = bb.psA if hc % 2 == 0 else bb.psB
                    sc.activation(htile(hc % 2), psh[:, :], AF.Silu,
                                  bias=bb.b1_sb[:, e * HC + hc: e * HC + hc + 1]
                                  ).then_inc(S(bb, "se"), 1)

        @blk.vector
        def _(v):
            for g in range(NTG):
                if g >= 2:
                    v.wait_ge(S(bb, "acc"), ba + (g - 1) * 16)
                elif e > 0:
                    v.wait_ge(S(bb, "acc"), ba - (1 - g) * 16)
                for ts in range(TT):
                    v.wait_ge(S(bb, "peo"), bo + g * TT + ts + 1)
                    pso = bb.psC if ts == 0 else bb.psD
                    col = (g * TT + ts) * NE + e
                    v.tensor_scalar(
                        bb.tmp_sb[:, (g % 2) * TT * D + ts * D: (g % 2) * TT * D + (ts + 1) * D],
                        pso[:, :], bb.s_sb[:, col:col + 1], None,
                        ALU.mult).then_inc(S(bb, "ve"), 1)

        @blk.gpsimd
        def _(gp):
            gp.wait_ge(S(bb, "x"), x_done)
            for g in range(NTG):
                gp.wait_ge(S(bb, "ve"), bv + (g + 1) * TT)
                if e > 0 or g > 0:
                    gp.wait_ge(S(bb, "acc"), ba + g * 16)
                gp.dma_start(
                    out=bb.y_scr[g * TG:(g + 1) * TG, :].rearrange(
                        "(tt p) d -> p tt d", p=128),
                    in_=bb.tmp_sb[:, (g % 2) * TT * D:((g % 2) + 1) * TT * D
                                  ].rearrange("p (tt d) -> p tt d", tt=TT),
                    accum_op=(ALU.bypass if e == 0 else ALU.add),
                ).then_inc(S(bb, "acc"), 16)

    bb.cnt["ul"] += NTG * 16
    bb.cnt["peh"] += NTG * HC
    bb.cnt["peo"] += NTG * TT
    bb.cnt["se"] += NTG * HC
    bb.cnt["ve"] += NTG * TT
    bb.cnt["acc"] += NTG * 16


def emit_postadd(bb, nc, srcA, dst, final):
    bx, bve = bb.cnt["x"], bb.cnt["ve"]
    bst = bb.cnt["st"]
    with nc.Block() as blk:
        @blk.sync
        def _(sy):
            sy.wait_ge(S(bb, "acc"), bb.cnt["acc"])
            for g in range(NTG):
                if g >= 1:
                    sy.wait_ge(S(bb, "ve"), bve + g)
                sy.dma_start(
                    out=bb.x_sb.rearrange("p (tt d) -> p tt d", tt=TT),
                    in_=srcA[g * TG:(g + 1) * TG, :].rearrange("(tt p) d -> p tt d", p=128),
                ).then_inc(S(bb, "x"), 16)
                sy.dma_start(
                    out=bb.tmp_sb[:, 0:TT * D].rearrange("p (tt d) -> p tt d", tt=TT),
                    in_=bb.y_scr[g * TG:(g + 1) * TG, :].rearrange("(tt p) d -> p tt d", p=128),
                ).then_inc(S(bb, "x"), 16)
                sy.wait_ge(S(bb, "ve"), bve + g + 1)
                sy.dma_start(
                    out=dst[g * TG:(g + 1) * TG, :].rearrange("(tt p) d -> p tt d", p=128),
                    in_=bb.tmp_sb[:, TT * D:2 * TT * D].rearrange("p (tt d) -> p tt d", tt=TT),
                ).then_inc(S(bb, "st"), 16)
            if final:
                sy.wait_ge(S(bb, "st"), bst + NTG * 16)

        @blk.vector
        def _(v):
            for g in range(NTG):
                v.wait_ge(S(bb, "x"), bx + (g + 1) * 32)
                if g >= 1:
                    v.wait_ge(S(bb, "st"), bst + g * 16)
                v.tensor_add(bb.tmp_sb[:, TT * D:2 * TT * D],
                             bb.x_sb[:, :], bb.tmp_sb[:, 0:TT * D]).then_inc(S(bb, "ve"), 1)

    bb.cnt["x"] += NTG * 32
    bb.cnt["ve"] += NTG
    bb.cnt["st"] += NTG * 16


def emit_projpass(bb, nc):
    bu, bw = bb.cnt["ul"], bb.cnt["w"]
    bpe, bse, bst = bb.cnt["pet"], bb.cnt["se"], bb.cnt["st"]
    u_done = bb.cnt["u"]
    outs = [(bb.vin_cc, None, 0), (bb.xb2_scr, None, 1)]

    with nc.Block() as blk:
        @blk.gpsimd
        def _(g):
            g.dma_start(out=bb.w1sb[:, 0:DC * DC * 128],
                        in_=bb.wvt).then_inc(S(bb, "w"), 16)
            g.dma_start(out=bb.w1sb[:, DC * DC * 128:2 * DC * DC * 128],
                        in_=bb.wx2t).then_inc(S(bb, "w"), 16)

        @blk.sync
        def _(sy):
            sy.wait_ge(S(bb, "u"), u_done)
            for g in range(NTG):
                if g >= 2:
                    sy.wait_ge(S(bb, "pet"), bpe + (g - 1) * 16)
                sy.dma_start(
                    out=bb.uT_sb[:, (g % 2) * DC * TG:((g % 2) + 1) * DC * TG].rearrange("p (dc t) -> p dc t", dc=DC),
                    in_=bb.u_scr[:, g * TG:(g + 1) * TG].rearrange(
                        "(dc p) t -> p dc t", p=128),
                ).then_inc(S(bb, "ul"), 16)
                for p_i, (dstD, _b, _o) in enumerate(outs):
                    for cc in range(DC):
                        k = p_i * DC + cc
                        sy.wait_ge(S(bb, "se"), bse + g * 16 + k + 1)
                        sy.dma_start(
                            out=dstD[cc * 128:(cc + 1) * 128, g * TG:(g + 1) * TG],
                            in_=bb.hu_sb[:, (k % 2) * TG + 2 * TG:(k % 2) * TG + 2 * TG + TG],
                        ).then_inc(S(bb, "st"), 16)

        @blk.tensor
        def _(t):
            t.wait_ge(S(bb, "w"), bw + 32)
            for g in range(NTG):
                t.wait_ge(S(bb, "ul"), bu + (g + 1) * 16)
                ub = (g % 2) * DC * TG
                for p_i in range(2):
                    for cc in range(DC):
                        k = p_i * DC + cc
                        if g * 16 + k >= 2:
                            t.wait_ge(S(bb, "se"), bse + g * 16 + k - 1)
                        psp = bb.psA if k % 2 == 0 else bb.psB
                        for dc in range(DC):
                            wt = bb.w1sb[:, (p_i * 64 + cc * DC + dc) * 128:
                                         (p_i * 64 + cc * DC + dc + 1) * 128]
                            t.ldweights(wt)
                            t.matmul(psp[:, :], wt,
                                     bb.uT_sb[:, ub + dc * TG: ub + (dc + 1) * TG],
                                     start=(dc == 0), stop=False)
                        br = bb.bvr_sb if p_i == 0 else bb.bx2r_sb
                        t.ldweights(br[:, cc * 128:(cc + 1) * 128])
                        t.matmul(psp[:, :], br[:, cc * 128:(cc + 1) * 128],
                                 bb.ones_sb[:, 0:TG], start=False,
                                 stop=True).then_inc(S(bb, "pet"), 1)

        @blk.scalar
        def _(sc):
            for g in range(NTG):
                for p_i in range(2):
                    for cc in range(DC):
                        k = p_i * DC + cc
                        sc.wait_ge(S(bb, "pet"), bpe + g * 16 + k + 1)
                        if g * 16 + k >= 2:
                            sc.wait_ge(S(bb, "st"), bst + (g * 16 + k - 1) * 16)
                        psp = bb.psA if k % 2 == 0 else bb.psB
                        sc.activation(
                            bb.hu_sb[:, (k % 2) * TG + 2 * TG:(k % 2) * TG + 2 * TG + TG],
                            psp[:, :], AF.Copy).then_inc(S(bb, "se"), 1)

    bb.cnt["w"] += 32
    bb.cnt["ul"] += NTG * 16
    bb.cnt["pet"] += NTG * 16
    bb.cnt["se"] += NTG * 16
    bb.cnt["st"] += NTG * 16 * 16



RG_PAIRS = [[0, 1], [2, 3], [4, 5], [6, 7]]


def emit_hysetup(bb, nc):
    st_done = bb.cnt["st"]
    bpo = bb.cnt["po"]
    bph = bb.cnt["peh"]
    bfh = bb.cnt["fh"]
    bfa = bb.cnt["fa"]
    bfst = bb.cnt["fst"]
    bfpe = bb.cnt["fpe"]
    bfve_pos = bb.cnt["fve"]
    bfve = bb.cnt["fve"] + 16

    with nc.Block() as blk:
        @blk.tensor
        def _(t):
            for c in range(16):
                if c >= 2:
                    t.wait_ge(S(bb, "po"), bpo + c - 1)
                t.wait_ge(S(bb, "fve"), bfve_pos + c + 1)
                psx = bb.psC if c % 2 == 0 else bb.psD
                t.matmul(psx[0:64, 0:512], bb.fw1_sb[:, :],
                         bb.pos_sb[0:1, (c % 2) * 512:(c % 2) * 512 + 512],
                         start=True, stop=True).then_inc(S(bb, "peh"), 1)
            for c in range(16):
                t.wait_ge(S(bb, "po"), bpo + c + 1)
                t.wait_ge(S(bb, "po"), bpo + (16 + c - 1 if c >= 2 else 15 + c))
                psx = bb.psC if c % 2 == 0 else bb.psD
                t.ldweights(bb.fw2_sb[:, :])
                t.matmul(psx[:, 0:512], bb.fw2_sb[:, :],
                         bb.h1_sb[:, c * 512:(c + 1) * 512],
                         start=True, stop=True).then_inc(S(bb, "peh"), 1)
            for ccl in range(4):
                for c in range(16):
                    k = ccl * 16 + c
                    t.wait_ge(S(bb, "po"), bpo + 16 + c + 1)
                    t.wait_ge(S(bb, "po"), bpo + (32 + k - 1 if k >= 2 else 31 + k))
                    psx = bb.psC if k % 2 == 0 else bb.psD
                    t.ldweights(bb.fw3_sb[:, ccl * 128:(ccl + 1) * 128])
                    t.matmul(psx[:, 0:512],
                             bb.fw3_sb[:, ccl * 128:(ccl + 1) * 128],
                             bb.h2_sb[:, c * 512:(c + 1) * 512],
                             start=True, stop=False)
                    t.ldweights(bb.fb3_sb[:, ccl * 128:(ccl + 1) * 128])
                    t.matmul(psx[:, 0:512], bb.fb3_sb[:, ccl * 128:(ccl + 1) * 128],
                             bb.ones_sb[:, :], start=False,
                             stop=True).then_inc(S(bb, "peh"), 1)
            for g2 in range(NFG):
                t.wait_ge(S(bb, "fa"), bfa + (g2 + 1) * 16)
                if g2 == 0:
                    t.wait_ge(S(bb, "po"), bpo + 96)
                if g2 >= 1:
                    t.wait_ge(S(bb, "fve"), bfve + 2 * g2)
                for s in range(FG):
                    at = bb.fA(g2 % 2)[:, s * 128:(s + 1) * 128]
                    t.ldweights(at)
                    t.matmul(bb.psC[:, s * 128:(s + 1) * 128], at,
                             bb.ww1_sb[:, 0:128], start=True, stop=True)
                    ins = t.matmul(bb.psC[:, 512 + s * 128:512 + (s + 1) * 128], at,
                                   bb.ww1_sb[:, 128:256], start=True, stop=True)
                ins.then_inc(S(bb, "fpe"), 1)
                t.wait_ge(S(bb, "fve"), bfve + 2 * g2 + 1)
                fCb = bb.fC(g2 % 2)
                rhs3 = fCb.rearrange("p (t s q) -> p s t q", t=2, s=FG, q=128)
                ins = None
                for s in range(FG):
                    t.ldweights(bb.coss_sb[:, :])
                    t.matmul(bb.psD[:, s * 256:(s + 1) * 256], bb.coss_sb[:, :],
                             rhs3[:, s, :, :], start=True, stop=False)
                    t.ldweights(bb.sins_sb[:, :])
                    t.matmul(bb.psD[:, s * 256:s * 256 + 128], bb.sins_sb[:, :],
                             fCb[:, 512 + s * 128:512 + (s + 1) * 128],
                             start=False, stop=False)
                    t.ldweights(bb.nsins_sb[:, :])
                    ins = t.matmul(bb.psD[:, s * 256 + 128:s * 256 + 256],
                                   bb.nsins_sb[:, :],
                                   fCb[:, s * 128:(s + 1) * 128],
                                   start=False, stop=True)
                ins.then_inc(S(bb, "fpe"), 1)

        @blk.scalar
        def _(sc):
            for c in range(16):
                sc.wait_ge(S(bb, "peh"), bph + c + 1)
                psx = bb.psC if c % 2 == 0 else bb.psD
                sc.activation(bb.h1_sb[:, c * 512:(c + 1) * 512], psx[0:64, 0:512],
                              AF.Gelu, bias=bb.fb1_sb[:, 0:1]).then_inc(S(bb, "po"), 1)
            for c in range(16):
                sc.wait_ge(S(bb, "peh"), bph + 16 + c + 1)
                psx = bb.psC if c % 2 == 0 else bb.psD
                sc.activation(bb.h2_sb[:, c * 512:(c + 1) * 512], psx[:, 0:512],
                              AF.Gelu, bias=bb.fb2_sb[:, 0:1]).then_inc(S(bb, "po"), 1)
            for ccl in range(4):
                for c in range(16):
                    k = ccl * 16 + c
                    sc.wait_ge(S(bb, "peh"), bph + 32 + k + 1)
                    if k >= 2:
                        sc.wait_ge(S(bb, "fh"), bfh + (k - 1) * 16)
                    psx = bb.psC if k % 2 == 0 else bb.psD
                    sc.activation(bb.hu_sb[:, (k % 2) * 512:(k % 2) * 512 + 512],
                                  psx[:, 0:512], AF.Copy).then_inc(S(bb, "po"), 1)

        @blk.vector
        def _(v):
            for c in range(16):
                if c >= 2:
                    v.wait_ge(S(bb, "peh"), bph + c - 1)
                v.tensor_scalar(bb.pos_sb[0:1, (c % 2) * 512:(c % 2) * 512 + 512],
                                bb.io_sb[0:1, :], float(1.0 / 8191.0),
                                float(c * 512 / 8191.0), ALU.mult,
                                ALU.add).then_inc(S(bb, "fve"), 1)
            for g2 in range(NFG):
                v.wait_ge(S(bb, "fpe"), bfpe + 2 * g2 + 1)
                if g2 >= 2:
                    v.wait_ge(S(bb, "fpe"), bfpe + 2 * (g2 - 2) + 2)
                t0, t1 = bb.tmp_sb[:, 0:512], bb.tmp_sb[:, 512:1024]
                t2, t3 = bb.tmp_sb[:, 1024:1536], bb.tmp_sb[:, 1536:2048]
                fCb = bb.fC(g2 % 2)
                v.tensor_mul(t0, bb.psC[:, 0:512], bb.twr_sb[:, :])
                v.tensor_mul(t1, bb.psC[:, 512:1024], bb.twi_sb[:, :])
                v.tensor_mul(t2, bb.psC[:, 0:512], bb.twi_sb[:, :])
                v.tensor_mul(t3, bb.psC[:, 512:1024], bb.twr_sb[:, :])
                v.tensor_sub(fCb[:, 0:512], t0, t1)
                v.tensor_add(fCb[:, 512:1024], t2, t3)
                v.tensor_copy(bb.misc[:, 126:127], bb.misc[:, 127:128])
                v.tensor_copy(bb.misc[:, 126:127],
                              bb.misc[:, 127:128]).then_inc(S(bb, "fve"), 1)
                v.wait_ge(S(bb, "fpe"), bfpe + 2 * g2 + 2)
                if g2 >= 2:
                    v.wait_ge(S(bb, "fst"), bfst + (g2 - 1) * 16)
                v.tensor_copy(bb.fS(g2 % 2), bb.psD[:, :]).then_inc(S(bb, "fve"), 1)

        @blk.sync
        def _(sy):
            for ccl in range(4):
                for c in range(16):
                    k = ccl * 16 + c
                    sy.wait_ge(S(bb, "po"), bpo + 32 + k + 1)
                    sy.dma_start(
                        out=bb.filt_scr[ccl * 128:(ccl + 1) * 128, c * 512:(c + 1) * 512],
                        in_=bb.hu_sb[:, (k % 2) * 512:(k % 2) * 512 + 512],
                    ).then_inc(S(bb, "fh"), 16)
            for g2 in range(NFG):
                ccl = (4 * g2) // 128
                sy.wait_ge(S(bb, "fh"), bfh + 256 * (ccl + 1))
                if g2 >= 2:
                    sy.wait_ge(S(bb, "fpe"), bfpe + 2 * (g2 - 2) + 1)
                sy.dma_start(
                    out=bb.fA(g2 % 2).rearrange("p (s n2) -> p s n2", s=FG),
                    in_=bb.filt_scr[4 * g2:4 * g2 + 4, :].rearrange(
                        "s (n1 n2) -> n1 s n2", n1=64),
                ).then_inc(S(bb, "fa"), 16)
                sy.wait_ge(S(bb, "fve"), bfve + 2 * g2 + 2)
                sy.dma_start(
                    out=bb.hspec_scr[4 * g2:4 * g2 + 4, :].rearrange(
                        "s (p q) -> p s q", p=128),
                    in_=bb.fS(g2 % 2).rearrange("p (s q) -> p s q", s=FG),
                ).then_inc(S(bb, "fst"), 16)

    bb.cnt["peh"] += 96
    bb.cnt["po"] += 96
    bb.cnt["fh"] += 64 * 16
    bb.cnt["fa"] += NFG * 16
    bb.cnt["fst"] += NFG * 16
    bb.cnt["fpe"] += NFG * 2
    bb.cnt["fve"] += NFG * 2 + 16


def emit_ccv(bb, nc):
    st_done = bb.cnt["st"]
    with nc.Block() as blk:
        @blk.gpsimd
        def _(g):
            g.wait_ge(S(bb, "st"), st_done)
            g.collective_compute(
                "AllGather", ALU.bypass, replica_groups=RG_PAIRS,
                ins=[bb.vin_cc.opt()], outs=[bb.vout_cc.opt()],
            ).then_inc(S(bb, "cc"), 1)
            g.wait_ge(S(bb, "cc"), 1)
    bb.cnt["cc"] += 1


def emit_shortconv(bb, nc):
    bx, bve, bst = bb.cnt["x"], bb.cnt["ve"], bb.cnt["st"]
    vre = bb.vout_cc.rearrange("(s c) t -> c s t", s=2)
    with nc.Block() as blk:
        @blk.sync
        def _(sy):
            sy.wait_ge(S(bb, "cc"), 1)
            pid = nc.partition_id(engines=[ET.SP])
            par = pid % 2
            for cc in range(4):
                if cc >= 1:
                    sy.wait_ge(S(bb, "ve"), bve + cc)
                with sy.If(par < 1):
                    sy.dma_start(out=bb.vrow[:, 1:L + 1].rearrange("p (s t) -> p s t", s=2),
                                 in_=vre[cc * 128:(cc + 1) * 128, :, :]).then_inc(S(bb, "x"), 16)
                with sy.Else():
                    sy.dma_start(out=bb.vrow[:, 1:L + 1].rearrange("p (s t) -> p s t", s=2),
                                 in_=vre[512 + cc * 128:512 + (cc + 1) * 128, :, :]
                                 ).then_inc(S(bb, "x"), 16)
                sy.wait_ge(S(bb, "ve"), bve + cc + 1)
                sy.dma_start(out=bb.vpr_scr[cc * 128:(cc + 1) * 128, :],
                             in_=bb.vrow2[:, :]).then_inc(S(bb, "st"), 16)

        @blk.vector
        def _(v):
            v.memset(bb.vrow[:, 0:1], 0.0)
            v.memset(bb.vrow[:, L + 1:L + 2], 0.0)
            for cc in range(4):
                v.wait_ge(S(bb, "x"), bx + (cc + 1) * 16)
                if cc >= 1:
                    v.wait_ge(S(bb, "st"), bst + cc * 16)
                v.tensor_scalar(bb.vrow2[:, :], bb.vrow[:, 0:L],
                                bb.swl_sb[:, cc * 3:cc * 3 + 1],
                                bb.sbl_sb[:, cc:cc + 1], ALU.mult, ALU.add)
                v.tensor_copy(bb.misc[:, 126:127], bb.misc[:, 127:128])
                v.scalar_tensor_tensor(bb.vrow2[:, :], bb.vrow[:, 1:L + 1],
                                       bb.swl_sb[:, cc * 3 + 1:cc * 3 + 2],
                                       bb.vrow2[:, :], ALU.mult, ALU.add)
                v.tensor_copy(bb.misc[:, 126:127], bb.misc[:, 127:128])
                v.scalar_tensor_tensor(bb.vrow2[:, :], bb.vrow[:, 2:L + 2],
                                       bb.swl_sb[:, cc * 3 + 2:cc * 3 + 3],
                                       bb.vrow2[:, :], ALU.mult, ALU.add
                                       ).then_inc(S(bb, "ve"), 1)

    bb.cnt["x"] += 4 * 16
    bb.cnt["ve"] += 4
    bb.cnt["st"] += 4 * 16


def emit_vfft(bb, nc):
    NIT = NFG + 2
    sched = {"pe": [], "ve": [], "po": [], "A": [], "H": [], "Y": []}
    for g in range(NIT):
        if g < NFG:
            sched["A"].append(("A", g)); sched["H"].append(("H", g))
            sched["pe"].append(("s1", g)); sched["pe"].append(("s2", g))
            sched["ve"].append(("tw", g)); sched["ve"].append(("evacV", g))
            sched["po"].append(("ptw", g))
        if 1 <= g <= NFG:
            sched["pe"].append(("iA", g - 1))
            sched["ve"].append(("itw", g - 1))
        if 2 <= g <= NFG + 1:
            sched["pe"].append(("iB", g - 2))
            sched["ve"].append(("yevac", g - 2))
            sched["Y"].append(("Y", g - 2))
    idx = {}
    for eng in sched:
        for i, key in enumerate(sched[eng]):
            idx[(eng,) + key] = i + 1
    ve_end = [0] * (NIT + 1)
    cnt_v = 0
    for g in range(NIT):
        n_in_it = (2 if g < NFG else 0) + (1 if 1 <= g <= NFG else 0) + (1 if 2 <= g <= NFG + 1 else 0)
        cnt_v += n_in_it
        ve_end[g + 1] = cnt_v

    bfa, bfh, bfst = bb.cnt["fa"], bb.cnt["fh"], bb.cnt["fst"]
    bfpe, bfve, bfpo = bb.cnt["fpe"], bb.cnt["fve"], bb.cnt["fpo"]
    st_short = bb.cnt["st"]

    def C(eng, st, g):
        return idx[(eng, st, g)]

    with nc.Block() as blk:
        @blk.sync
        def _(sy):
            sy.wait_ge(S(bb, "st"), st_short)
            sy.wait_ge(S(bb, "fst"), bfst)
            for g in range(NIT):
                if g < NFG:
                    if g >= 2:
                        sy.wait_ge(S(bb, "fpe"), bfpe + C("pe", "s1", g - 2))
                    sy.dma_start(
                        out=bb.fA(g % 2).rearrange("p (s n2) -> p s n2", s=FG),
                        in_=bb.vpr_scr[4 * g:4 * g + 4, :].rearrange(
                            "s (n1 n2) -> n1 s n2", n1=64),
                    ).then_inc(S(bb, "fa"), 16)
                    if g >= 2:
                        sy.wait_ge(S(bb, "fpo"), bfpo + C("po", "ptw", g - 2))
                    sy.dma_start(
                        out=bb.fH(g % 2).rearrange("p (s q) -> p s q", s=FG),
                        in_=bb.hspec_scr[4 * g:4 * g + 4, :].rearrange(
                            "s (p q) -> p s q", p=128),
                    ).then_inc(S(bb, "fh"), 16)
                if 2 <= g:
                    j = g - 2
                    if j < NFG:
                        sy.wait_ge(S(bb, "fve"), bfve + C("ve", "yevac", j))
                        sy.dma_start(
                            out=bb.cin_cc[4 * j * 64:(4 * j + 4) * 64, :].rearrange(
                                "(s n1) n2 -> n1 s n2", s=4),
                            in_=bb.fy(j % 2).rearrange("p (s q) -> p s q", s=FG),
                        ).then_inc(S(bb, "fst"), 16)

        @blk.tensor
        def _(t):
            for g in range(NIT):
                if g < NFG:
                    t.wait_ge(S(bb, "fa"), bfa + 16 * (g + 1))
                    if g >= 1:
                        t.wait_ge(S(bb, "fve"), bfve + ve_end[g])
                    ins = None
                    for s in range(FG):
                        at = bb.fA(g % 2)[:, s * 128:(s + 1) * 128]
                        t.ldweights(at)
                        t.matmul(bb.psC[:, s * 128:(s + 1) * 128], at,
                                 bb.ww1_sb[:, 0:128], start=True, stop=True)
                        ins = t.matmul(bb.psC[:, 512 + s * 128:512 + (s + 1) * 128], at,
                                       bb.ww1_sb[:, 128:256], start=True, stop=True)
                    ins.then_inc(S(bb, "fpe"), 1)
                    t.wait_ge(S(bb, "fve"), bfve + C("ve", "tw", g))
                    fCb = bb.fC(g % 2)
                    rhs3 = fCb.rearrange("p (t s q) -> p s t q", t=2, s=FG, q=128)
                    ins = None
                    for s in range(FG):
                        t.ldweights(bb.coss_sb[:, :])
                        t.matmul(bb.psD[:, s * 256:(s + 1) * 256], bb.coss_sb[:, :],
                                 rhs3[:, s, :, :], start=True, stop=False)
                        t.ldweights(bb.sins_sb[:, :])
                        t.matmul(bb.psD[:, s * 256:s * 256 + 128], bb.sins_sb[:, :],
                                 fCb[:, 512 + s * 128:512 + (s + 1) * 128],
                                 start=False, stop=False)
                        t.ldweights(bb.nsins_sb[:, :])
                        ins = t.matmul(bb.psD[:, s * 256 + 128:s * 256 + 256],
                                       bb.nsins_sb[:, :],
                                       fCb[:, s * 128:(s + 1) * 128],
                                       start=False, stop=True)
                    ins.then_inc(S(bb, "fpe"), 1)
                if 1 <= g <= NFG:
                    j = g - 1
                    t.wait_ge(S(bb, "fpo"), bfpo + C("po", "ptw", j))
                    fPb = bb.fP(j % 2)
                    ins = None
                    for s in range(FG):
                        pr = fPb[:, s * 256:s * 256 + 128]
                        pi = fPb[:, s * 256 + 128:s * 256 + 256]
                        t.ldweights(pr)
                        t.matmul(bb.psC[:, s * 128:(s + 1) * 128], pr,
                                 bb.wwA_sb[:, 0:128], start=True, stop=False)
                        t.matmul(bb.psC[:, 512 + s * 128:512 + (s + 1) * 128], pr,
                                 bb.wwA_sb[:, 128:256], start=True, stop=False)
                        t.ldweights(pi)
                        t.matmul(bb.psC[:, s * 128:(s + 1) * 128], pi,
                                 bb.wwB_sb[:, 0:128], start=False, stop=True)
                        ins = t.matmul(bb.psC[:, 512 + s * 128:512 + (s + 1) * 128], pi,
                                       bb.wwB_sb[:, 128:256], start=False, stop=True)
                    ins.then_inc(S(bb, "fpe"), 1)
                if 2 <= g <= NFG + 1:
                    j = g - 2
                    t.wait_ge(S(bb, "fve"), bfve + C("ve", "itw", j))
                    fGb = bb.fG(j % 2)
                    slots = [bb.psA[0:64, 0:128], bb.psA[0:64, 128:256],
                             bb.psB[0:64, 0:128], bb.psB[0:64, 128:256]]
                    ins = None
                    for s in range(FG):
                        t.ldweights(bb.cosb_sb[:, :])
                        t.matmul(slots[s], bb.cosb_sb[:, :],
                                 fGb[:, s * 128:(s + 1) * 128], start=True, stop=False)
                        t.ldweights(bb.nsinb_sb[:, :])
                        ins = t.matmul(slots[s], bb.nsinb_sb[:, :],
                                       fGb[:, 512 + s * 128:512 + (s + 1) * 128],
                                       start=False, stop=True)
                    ins.then_inc(S(bb, "fpe"), 1)

        @blk.vector
        def _(v):
            t0, t1 = bb.tmp_sb[:, 0:512], bb.tmp_sb[:, 512:1024]
            t2, t3 = bb.tmp_sb[:, 1024:1536], bb.tmp_sb[:, 1536:2048]
            for g in range(NIT):
                if g < NFG:
                    v.wait_ge(S(bb, "fpe"), bfpe + C("pe", "s1", g))
                    if g >= 2:
                        v.wait_ge(S(bb, "fpe"), bfpe + C("pe", "s2", g - 2))
                    fCb = bb.fC(g % 2)
                    v.tensor_mul(t0, bb.psC[:, 0:512], bb.twr_sb[:, :])
                    v.tensor_mul(t1, bb.psC[:, 512:1024], bb.twi_sb[:, :])
                    v.tensor_mul(t2, bb.psC[:, 0:512], bb.twi_sb[:, :])
                    v.tensor_mul(t3, bb.psC[:, 512:1024], bb.twr_sb[:, :])
                    v.tensor_sub(fCb[:, 0:512], t0, t1)
                    v.tensor_add(fCb[:, 512:1024], t2, t3)
                    v.tensor_copy(bb.misc[:, 126:127], bb.misc[:, 127:128])
                    v.tensor_copy(bb.misc[:, 126:127],
                                  bb.misc[:, 127:128]).then_inc(S(bb, "fve"), 1)
                    v.wait_ge(S(bb, "fpe"), bfpe + C("pe", "s2", g))
                    if g >= 2:
                        v.wait_ge(S(bb, "fpo"), bfpo + C("po", "ptw", g - 2))
                    v.tensor_copy(bb.fS(g % 2), bb.psD[:, :]).then_inc(S(bb, "fve"), 1)
                if 1 <= g <= NFG:
                    j = g - 1
                    v.wait_ge(S(bb, "fpe"), bfpe + C("pe", "iA", j))
                    if j >= 2:
                        v.wait_ge(S(bb, "fpe"), bfpe + C("pe", "iB", j - 2))
                    fGb = bb.fG(j % 2)
                    v.tensor_mul(t0, bb.psC[:, 0:512], bb.twri_sb[:, :])
                    v.tensor_mul(t1, bb.psC[:, 512:1024], bb.twii_sb[:, :])
                    v.tensor_mul(t2, bb.psC[:, 0:512], bb.twii_sb[:, :])
                    v.tensor_mul(t3, bb.psC[:, 512:1024], bb.twri_sb[:, :])
                    v.tensor_sub(fGb[:, 0:512], t0, t1)
                    v.tensor_add(fGb[:, 512:1024], t2, t3)
                    v.tensor_copy(bb.misc[:, 126:127], bb.misc[:, 127:128])
                    v.tensor_copy(bb.misc[:, 126:127],
                                  bb.misc[:, 127:128]).then_inc(S(bb, "fve"), 1)
                if 2 <= g <= NFG + 1:
                    j = g - 2
                    v.wait_ge(S(bb, "fpe"), bfpe + C("pe", "iB", j))
                    if j >= 2:
                        v.wait_ge(S(bb, "fst"), bfst + 16 * (j - 1))
                    v.tensor_copy(bb.fy(j % 2)[:, 0:256], bb.psA[0:64, :])
                    v.tensor_copy(bb.fy(j % 2)[:, 256:512], bb.psB[0:64, :])
                    v.tensor_copy(bb.misc[:, 126:127], bb.misc[:, 127:128])
                    v.tensor_copy(bb.misc[:, 126:127],
                                  bb.misc[:, 127:128]).then_inc(S(bb, "fve"), 1)

        @blk.gpsimd
        def _(p):
            s0, s1_ = bb.tmp_sb[:, 2048:2560], bb.tmp_sb[:, 2560:3072]
            s2_, s3_ = bb.tmp_sb[:, 3072:3584], bb.tmp_sb[:, 3584:4096]
            for g in range(NFG):
                p.wait_ge(S(bb, "fve"), bfve + C("ve", "evacV", g))
                p.wait_ge(S(bb, "fh"), bfh + 16 * (g + 1))
                if g >= 2:
                    p.wait_ge(S(bb, "fpe"), bfpe + C("pe", "iA", g - 2))
                fSb, fHb, fPb = bb.fS(g % 2), bb.fH(g % 2), bb.fP(g % 2)
                vr = fSb.rearrange("p (s t q) -> p t s q", s=FG, t=2, q=128)
                hr = fHb.rearrange("p (s t q) -> p t s q", s=FG, t=2, q=128)
                pr = fPb.rearrange("p (s t q) -> p t s q", s=FG, t=2, q=128)
                s0v = s0.rearrange("p (s q) -> p s q", s=FG)
                s1v = s1_.rearrange("p (s q) -> p s q", s=FG)
                s2v = s2_.rearrange("p (s q) -> p s q", s=FG)
                s3v = s3_.rearrange("p (s q) -> p s q", s=FG)
                p.tensor_mul(s0v, vr[:, 0, :, :], hr[:, 0, :, :])
                p.tensor_mul(s1v, vr[:, 1, :, :], hr[:, 1, :, :])
                p.tensor_mul(s2v, vr[:, 0, :, :], hr[:, 1, :, :])
                p.tensor_mul(s3v, vr[:, 1, :, :], hr[:, 0, :, :])
                p.tensor_sub(pr[:, 0, :, :], s0v, s1v)
                p.tensor_add(pr[:, 1, :, :], s2v, s3v).then_inc(S(bb, "fpo"), 1)

    bb.cnt["fa"] += NFG * 16
    bb.cnt["fh"] += NFG * 16
    bb.cnt["fst"] += NFG * 16
    bb.cnt["fpe"] += len(sched["pe"])
    bb.cnt["fve"] += len(sched["ve"])
    bb.cnt["fpo"] += NFG


def emit_ccconv(bb, nc):
    fst_done = bb.cnt["fst"]
    with nc.Block() as blk:
        @blk.gpsimd
        def _(g):
            g.wait_ge(S(bb, "fst"), fst_done)
            g.collective_compute(
                "AllGather", ALU.bypass, replica_groups=RG_PAIRS,
                ins=[bb.cin_cc.opt()], outs=[bb.cout_cc.opt()],
            ).then_inc(S(bb, "cc"), 1)
            g.wait_ge(S(bb, "cc"), 2)
    bb.cnt["cc"] += 1


def emit_hyfinish(bb, nc, dst, final):
    bx, bve, bst = bb.cnt["x"], bb.cnt["ve"], bb.cnt["peo"]
    bso = bb.cnt["st"]
    bw = bb.cnt["w"]
    st_prev = bb.cnt["st"]
    cre = bb.cout_cc.rearrange("(c n1) n2 -> c n1 n2", n1=64)

    with nc.Block() as blk:
        @blk.gpsimd
        def _(g):
            g.dma_start(out=bb.w2sb[:, 0:DC * D].rearrange("p (cc d) -> p cc d", cc=DC),
                        in_=bb.owr.rearrange("(cc p) d -> p cc d", p=128)
                        ).then_inc(S(bb, "w"), 16)

        @blk.sync
        def _(sy):
            sy.wait_ge(S(bb, "cc"), 2)
            sy.wait_ge(S(bb, "st"), st_prev)
            pid = nc.partition_id(engines=[ET.SP])
            par = pid % 2
            for g in range(NTG):
                if g >= 1:
                    sy.wait_ge(S(bb, "ve"), bve + g * 3)
                sy.dma_start(
                    out=bb.x_sb.rearrange("p (tt d) -> p tt d", tt=TT),
                    in_=bb.x1_scr[g * TG:(g + 1) * TG, :].rearrange(
                        "(tt p) d -> p tt d", p=128),
                ).then_inc(S(bb, "x"), 16)
                sy.dma_start(
                    out=bb.uT_sb[:, 0:2048].rearrange("p (cc t) -> p cc t", cc=DC),
                    in_=bb.xb2_scr[:, g * TG:(g + 1) * TG].rearrange(
                        "(cc p) t -> p cc t", p=128),
                ).then_inc(S(bb, "x"), 16)
                with sy.If(par < 1):
                    for cc in range(DC):
                        sy.dma_start(
                            out=bb.uT_sb[:, 2048 + cc * 256:2048 + (cc + 1) * 256],
                            in_=cre[cc * 128:(cc + 1) * 128, 2 * g:2 * g + 2, :].rearrange(
                                "c tt q -> c (tt q)"),
                        ).then_inc(S(bb, "x"), 16)
                with sy.Else():
                    for cc in range(DC):
                        sy.dma_start(
                            out=bb.uT_sb[:, 2048 + cc * 256:2048 + (cc + 1) * 256],
                            in_=cre[cc * 128:(cc + 1) * 128, 32 + 2 * g:32 + 2 * g + 2, :
                                    ].rearrange("c tt q -> c (tt q)"),
                        ).then_inc(S(bb, "x"), 16)
                sy.wait_ge(S(bb, "ve"), bve + g * 3 + 3)
                sy.dma_start(
                    out=dst[g * TG:(g + 1) * TG, :].rearrange("(tt p) d -> p tt d", p=128),
                    in_=bb.tmp_sb[:, TT * D:2 * TT * D].rearrange("p (tt d) -> p tt d", tt=TT),
                ).then_inc(S(bb, "st"), 16)
            if final:
                sy.wait_ge(S(bb, "st"), bso + NTG * 16)

        @blk.vector
        def _(v):
            for g in range(NTG):
                v.wait_ge(S(bb, "x"), bx + g * 160 + 160)
                if g >= 1:
                    v.wait_ge(S(bb, "peo"), bst + (g - 1) * 2 + 2)
                for cc in range(DC):
                    v.tensor_mul(bb.hu_sb[:, cc * 256:(cc + 1) * 256],
                                 bb.uT_sb[:, cc * 256:(cc + 1) * 256],
                                 bb.uT_sb[:, 2048 + cc * 256:2048 + (cc + 1) * 256])
                v.tensor_copy(bb.misc[:, 126:127], bb.misc[:, 127:128])
                v.tensor_copy(bb.misc[:, 126:127],
                              bb.misc[:, 127:128]).then_inc(S(bb, "ve"), 1)
                for tt in range(TT):
                    v.wait_ge(S(bb, "peo"), bst + g * 2 + tt + 1)
                    if g >= 1 and tt == 0:
                        v.wait_ge(S(bb, "st"), bso + g * 16)
                    pso = bb.psC if tt == 0 else bb.psD
                    v.tensor_add(bb.tmp_sb[:, TT * D + tt * D:TT * D + (tt + 1) * D],
                                 pso[:, :],
                                 bb.x_sb[:, tt * D:(tt + 1) * D]).then_inc(S(bb, "ve"), 1)

        @blk.tensor
        def _(t):
            t.wait_ge(S(bb, "w"), bw + 16)
            for g in range(NTG):
                t.wait_ge(S(bb, "ve"), bve + g * 3 + 1)
                for tt in range(TT):
                    pso = bb.psC if tt == 0 else bb.psD
                    if g >= 1:
                        t.wait_ge(S(bb, "ve"), bve + (g - 1) * 3 + 1 + tt + 1)
                    for cc in range(DC):
                        pt = bb.hu_sb[:, cc * 256 + tt * 128: cc * 256 + (tt + 1) * 128]
                        t.ldweights(pt)
                        t.matmul(pso[:, 0:512], pt,
                                 bb.w2sb[:, cc * D:cc * D + 512],
                                 start=(cc == 0), stop=False)
                        t.matmul(pso[:, 512:1024], pt,
                                 bb.w2sb[:, cc * D + 512:(cc + 1) * D],
                                 start=(cc == 0), stop=False)
                    t.ldweights(bb.ones_sb[:, 0:128])
                    t.matmul(pso[:, 0:512], bb.ones_sb[:, 0:128],
                             bb.obr_sb[:, 0:512], start=False, stop=True)
                    t.matmul(pso[:, 512:1024], bb.ones_sb[:, 0:128],
                             bb.obr_sb[:, 512:1024], start=False,
                             stop=True).then_inc(S(bb, "peo"), 1)

    bb.cnt["w"] += 16
    bb.cnt["x"] += NTG * 160
    bb.cnt["ve"] += NTG * 3
    bb.cnt["peo"] += NTG * 2
    bb.cnt["st"] += NTG * 16


def emit_tap(bb, nc, src_flat, flat=True):
    waits = [(n, bb.cnt[n]) for n in ["acc", "u", "st", "fst", "x", "fh", "fa"]]
    with nc.Block() as blk:
        @blk.gpsimd
        def _(g):
            for n, cval in waits:
                g.wait_ge(S(bb, n), cval)
            g.dma_start(out=bb.y_sh.rearrange("t d -> (t d)"),
                        in_=src_flat).then_inc(S(bb, "st"), 16)
            g.wait_ge(S(bb, "st"), bb.cnt["st"] + 16)
    bb.cnt["st"] += 16



def build_nc():
    nc = bass.Bass("TRN2", target_bir_lowering=False, enable_partition_id=True)
    bb = BB()
    bb.ctx = ExitStack()
    _alloc(bb, nc)

    emit_p0(bb, nc)
    emit_lnpass(bb, nc, bb.x_sh, gating=True)
    if DEBUG_STAGE == 4:
        emit_tap(bb, nc, bb.u_scr.rearrange("d t -> (d t)"), flat=True)
        bb.ctx.close(); nc.finalize(); return nc
    if DEBUG_STAGE == 5:
        with nc.Block() as blk5:
            @blk5.sync
            def _(sy):
                sy.wait_ge(S(bb, "ve"), bb.cnt["ve"])
                sy.dma_start(out=bb.y_sh[0:128, 0:256], in_=bb.s_sb[:, :]
                             ).then_inc(S(bb, "st"), 16)
                sy.dma_start(out=bb.y_sh[128:256, 0:128], in_=bb.misc[:, :]
                             ).then_inc(S(bb, "st"), 16)
                sy.wait_ge(S(bb, "st"), bb.cnt["st"] + 32)
        bb.cnt["st"] += 32
        bb.ctx.close(); nc.finalize(); return nc
    for e in range(NE):
        emit_wload(bb, nc, e)
        emit_expert(bb, nc, e)
    if DEBUG_STAGE == 3:
        emit_tap(bb, nc, bb.y_scr.rearrange("t d -> (t d)"), flat=True)
        bb.ctx.close(); nc.finalize(); return nc
    if DEBUG_STAGE == 1:
        emit_postadd(bb, nc, bb.x_sh, bb.y_sh, final=True)
        bb.ctx.close()
        nc.finalize()
        return nc
    emit_postadd(bb, nc, bb.x_sh, bb.x1_scr, final=False)
    emit_lnpass(bb, nc, bb.x1_scr, gating=False)
    emit_projpass(bb, nc)
    if DEBUG_STAGE == 6:
        emit_tap(bb, nc, bb.vin_cc.rearrange("d t -> (d t)"))
        bb.ctx.close(); nc.finalize(); return nc
    if DEBUG_STAGE == 7:
        emit_tap(bb, nc, bb.xb2_scr.rearrange("d t -> (d t)"))
        bb.ctx.close(); nc.finalize(); return nc
    emit_hysetup(bb, nc)
    emit_ccv(bb, nc)
    if DEBUG_STAGE == 9:
        emit_tap(bb, nc, bb.filt_scr.rearrange("c l -> (c l)"))
        bb.ctx.close(); nc.finalize(); return nc
    if DEBUG_STAGE == 10:
        emit_tap(bb, nc, bb.hspec_scr[0:128, :].rearrange("c l -> (c l)"))
        bb.ctx.close(); nc.finalize(); return nc
    emit_shortconv(bb, nc)
    if DEBUG_STAGE == 11:
        emit_tap(bb, nc, bb.vpr_scr.rearrange("c l -> (c l)"))
        bb.ctx.close(); nc.finalize(); return nc
    emit_vfft(bb, nc)
    if DEBUG_STAGE == 8:
        emit_tap(bb, nc, bb.cin_cc.rearrange("a b -> (a b)"))
        bb.ctx.close(); nc.finalize(); return nc
    emit_ccconv(bb, nc)
    if DEBUG_STAGE == 2:
        emit_hyfinish(bb, nc, bb.y_sh, final=True)
        bb.ctx.close()
        nc.finalize()
        return nc
    emit_hyfinish(bb, nc, bb.x1_scr, final=False)
    emit_lnpass(bb, nc, bb.x1_scr, gating=True)
    for e in range(NE):
        emit_wload(bb, nc, e)
        emit_expert(bb, nc, e)
    emit_postadd(bb, nc, bb.x1_scr, bb.y_sh, final=True)
    bb.ctx.close()
    nc.finalize()
    return nc


def kernel(**inputs):
    shared, per_core = _prep(inputs)
    nc = build_nc()
    in_maps = []
    for c in range(8):
        m = dict(shared)
        m.update(per_core[c])
        in_maps.append(m)
    res = run_bass_kernel_spmd(nc, in_maps, core_ids=list(range(8)))
    out = np.zeros((4, 8192, 1024), dtype=np.float32)
    for c in range(8):
        b, h = divmod(c, 2)
        out[b, 4096 * h:4096 * h + 4096, :] = res.results[c]["y_sh"]
    return out


# version 15
# speedup vs baseline: 1.0290x; 1.0290x over previous
import numpy as np
import ml_dtypes
from contextlib import ExitStack

import concourse.bass as bass
import concourse.mybir as mybir
from concourse.bass_utils import run_bass_kernel_spmd

F32 = mybir.dt.float32
BF16 = mybir.dt.bfloat16
I32 = mybir.dt.int32
AF = mybir.ActivationFunctionType
ALU = mybir.AluOpType
AX = mybir.AxisListType
ET = mybir.EngineType

D, DH, NE, T = 1024, 4096, 8, 4096
TG = 256
NTG = T // TG
TT = TG // 128
HC, DC = DH // 128, D // 128
L = 8192
FG = 4
NSEQ = 512
NFG = NSEQ // FG

DEBUG_STAGE = 0

MC_SUM, MC_SQ, MC_MEAN, MC_NEG, MC_Q, MC_VE, MC_LNV, MC_RSTD = 0, 4, 8, 12, 16, 20, 24, 28
MC_LG, MC_MX, MC_NMX, MC_EXP, MC_SUME, MC_WINV, MC_MSK = 32, 60, 62, 64, 80, 82, 88



def _tables():
    t = {}
    kk = np.arange(128)
    a1 = 2 * np.pi * (np.arange(64)[:, None] * kk[None, :]) / 128.0
    t["ww1"] = np.concatenate([np.cos(a1), -np.sin(a1)], 1).astype(np.float32)
    aT = 2 * np.pi * (np.arange(128)[:, None] * kk[None, :]) / 16384.0
    t["twr"] = np.tile(np.cos(aT), (1, FG)).astype(np.float32)
    t["twi"] = np.tile(-np.sin(aT), (1, FG)).astype(np.float32)
    t["twri"] = np.tile(np.cos(aT) / 16384.0, (1, FG)).astype(np.float32)
    t["twii"] = np.tile(np.sin(aT) / 16384.0, (1, FG)).astype(np.float32)
    a2 = 2 * np.pi * (np.arange(128)[:, None] * kk[None, :]) / 128.0
    COS, SIN = np.cos(a2), np.sin(a2)
    t["coss"], t["sins"], t["nsins"] = (COS.astype(np.float32),
                                        SIN.astype(np.float32), (-SIN).astype(np.float32))
    t["wwA"] = np.concatenate([COS, SIN], 1).astype(np.float32)
    t["wwB"] = np.concatenate([-SIN, COS], 1).astype(np.float32)
    aB = 2 * np.pi * (np.arange(128)[:, None] * np.arange(64)[None, :]) / 128.0
    t["cosb"] = np.cos(aB).astype(np.float32)
    t["nsinb"] = (-np.sin(aB)).astype(np.float32)
    t["ident"] = np.eye(128, dtype=np.float32)
    return t


def _prep(inputs):
    f32 = np.float32
    W1 = np.ascontiguousarray(np.asarray(inputs["L0_ffn_W1"], f32))
    W2 = np.ascontiguousarray(np.asarray(inputs["L0_ffn_W2"], f32))
    w1t = np.ascontiguousarray(W1.reshape(NE, DC, 128, HC, 128).transpose(0, 2, 3, 1, 4))
    b1 = np.asarray(inputs["L0_ffn_b1"], f32)
    b1g = np.ascontiguousarray(b1.reshape(NE, HC, 128).transpose(2, 0, 1).reshape(128, NE * HC))
    b2r = np.ascontiguousarray(np.asarray(inputs["L0_ffn_b2"], f32).reshape(1, NE * D))
    wg = np.ascontiguousarray(np.asarray(inputs["L0_ffn_Wg"], f32))
    bg = np.ascontiguousarray(np.asarray(inputs["L0_ffn_bg"], f32).reshape(1, NE))

    ipw = np.asarray(inputs["L0_attn_in_proj_w"], f32)
    ipb = np.asarray(inputs["L0_attn_in_proj_b"], f32)
    Wv, bv = ipw[:, 0::3], ipb[0::3]
    Wx2, bx2 = ipw[:, 2::3], ipb[2::3]

    def proj_tiles(W):
        tl = W.reshape(DC, 128, DC, 128).transpose(1, 2, 0, 3)
        return np.ascontiguousarray(tl.reshape(128, DC * DC * 128))

    shared = dict(
        w1t=w1t, w2r=W2, b1g=b1g, b2r=b2r, wg=wg, bg=bg,
        wvt=proj_tiles(Wv), wx2t=proj_tiles(Wx2),
        bvr=np.ascontiguousarray(bv.reshape(1, D)),
        bx2r=np.ascontiguousarray(bx2.reshape(1, D)),
        owr=np.ascontiguousarray(np.asarray(inputs["L0_attn_out_w"], f32)),
        obr=np.ascontiguousarray(np.asarray(inputs["L0_attn_out_b"], f32).reshape(1, D)),
        fw1=np.ascontiguousarray(np.asarray(inputs["L0_attn_fw1"], f32)),
        fb1g=np.ascontiguousarray(np.asarray(inputs["L0_attn_fb1"], f32).reshape(64, 1)),
        fw2=np.ascontiguousarray(np.asarray(inputs["L0_attn_fw2"], f32)),
        fb2g=np.ascontiguousarray(np.asarray(inputs["L0_attn_fb2"], f32).reshape(128, 1)),
    )
    shared.update(_tables())

    fw3 = np.asarray(inputs["L0_attn_fw3"], f32)
    fb3 = np.asarray(inputs["L0_attn_fb3"], f32)
    fw3o, fb3o = fw3[:, 1::2], fb3[1::2]
    sw = np.asarray(inputs["L0_attn_short_w"], f32)
    sbv = np.asarray(inputs["L0_attn_short_b"], f32)
    x = np.asarray(inputs["L0_attn_x"], f32)
    per_core = []
    for b in range(4):
        for h in range(2):
            c0 = 512 * h
            per_core.append(dict(
                x_sh=np.ascontiguousarray(x[b, 4096 * h:4096 * h + 4096, :]),
                fw3l=np.ascontiguousarray(fw3o[:, c0:c0 + 512]),
                fb3r=np.ascontiguousarray(fb3o[c0:c0 + 512].reshape(1, 512)),
                swl=np.ascontiguousarray(
                    sw[c0:c0 + 512, 0, :].reshape(4, 128, 3).transpose(1, 0, 2).reshape(128, 12)),
                sbl=np.ascontiguousarray(sbv[c0:c0 + 512].reshape(4, 128).T),
            ))
    return shared, per_core



class BB:
    pass


def S(bb, n):
    return bb.sems[n]


def _alloc(bb, nc):
    dram = lambda n, s, k="ExternalInput": nc.dram_tensor(n, s, F32, kind=k).ap()
    for n, s in [("x_sh", [T, D]), ("w1t", [NE, 128, HC, DC, 128]), ("w2r", [NE, DH, D]),
                 ("b1g", [128, NE * HC]), ("b2r", [1, NE * D]), ("wg", [D, NE]), ("bg", [1, NE]),
                 ("wvt", [128, DC * DC * 128]), ("wx2t", [128, DC * DC * 128]),
                 ("bvr", [1, D]), ("bx2r", [1, D]), ("owr", [D, D]), ("obr", [1, D]),
                 ("fw1", [1, 64]), ("fb1g", [64, 1]), ("fw2", [64, 128]), ("fb2g", [128, 1]),
                 ("fw3l", [128, 512]), ("fb3r", [1, 512]), ("swl", [128, 12]), ("sbl", [128, 4]),
                 ("ww1", [64, 256]), ("twr", [128, 512]), ("twi", [128, 512]),
                 ("twri", [128, 512]), ("twii", [128, 512]),
                 ("coss", [128, 128]), ("sins", [128, 128]), ("nsins", [128, 128]),
                 ("wwA", [128, 256]), ("wwB", [128, 256]),
                 ("cosb", [128, 64]), ("nsinb", [128, 64]), ("ident", [128, 128])]:
        setattr(bb, n, dram(n, s))
    bb.y_sh = dram("y_sh", [T, D], k="ExternalOutput")

    idram = lambda n, s, d: nc.dram_tensor(n, s, d, kind="Internal").ap()
    bb.u_scr = idram("u_scr", [D, T], BF16)
    bb.y_scr = idram("y_scr", [T, D], F32)
    bb.x1_scr = idram("x1_scr", [T, D], F32)
    bb.xb2_scr = idram("xb2_scr", [D, T], BF16)
    bb.vin_cc = idram("vin_cc", [D, T], BF16)
    bb.vout_cc = idram("vout_cc", [2 * D, T], BF16)
    bb.vpr_scr = idram("vpr_scr", [512, L], BF16)
    bb.filt_scr = idram("filt_scr", [512, L], BF16)
    bb.hspec_scr = idram("hspec_scr", [512, 256 * 128], BF16)
    bb.cin_cc = idram("cin_cc", [512 * 64, 128], BF16)
    bb.cout_cc = idram("cout_cc", [2 * 512 * 64, 128], BF16)

    sb = lambda n, s, d: bb.ctx.enter_context(nc.sbuf_tensor(n, s, d))
    bb.w1sb = sb("w1sb", [128, HC * DC * 128], BF16)
    bb.w2sb = sb("w2sb", [128, HC * D], BF16)
    bb.hu_sb = sb("hu_sb", [128, 4096], BF16)
    bb.uT_sb = sb("uT_sb", [128, 2 * DC * TG], BF16)
    bb.x_sb = sb("x_sb", [128, TT * D], F32)
    bb.tmp_sb = sb("tmp_sb", [128, 2 * TT * D], F32)
    bb.s_sb = sb("s_sb", [128, (T // 128) * NE], F32)
    bb.misc = sb("misc", [128, 128], F32)
    bb.b1_sb = sb("b1_sb", [128, NE * HC], F32)
    bb.b2e_sb = sb("b2e_sb", [1, D], BF16)
    bb.bg_sb = sb("bg_sb", [1, NE], BF16)
    bb.bvr_sb = sb("bvr_sb", [1, D], BF16)
    bb.bx2r_sb = sb("bx2r_sb", [1, D], BF16)
    bb.obr_sb = sb("obr_sb", [1, D], BF16)
    bb.wg_sb = sb("wg_sb", [128, DC * NE], BF16)
    bb.ones_sb = sb("ones_sb", [1, 512], BF16)
    bb.id_sb = sb("id_sb", [128, 128], BF16)
    bb.ww1_sb = sb("ww1_sb", [64, 256], BF16)
    bb.coss_sb = sb("coss_sb", [128, 128], BF16)
    bb.sins_sb = sb("sins_sb", [128, 128], BF16)
    bb.nsins_sb = sb("nsins_sb", [128, 128], BF16)
    bb.wwA_sb = sb("wwA_sb", [128, 256], BF16)
    bb.wwB_sb = sb("wwB_sb", [128, 256], BF16)
    bb.cosb_sb = sb("cosb_sb", [128, 64], BF16)
    bb.nsinb_sb = sb("nsinb_sb", [128, 64], BF16)
    bb.twr_sb = sb("twr_sb", [128, 512], F32)
    bb.twi_sb = sb("twi_sb", [128, 512], F32)
    bb.twri_sb = sb("twri_sb", [128, 512], F32)
    bb.twii_sb = sb("twii_sb", [128, 512], F32)
    bb.fw1_sb = sb("fw1_sb", [1, 64], F32)
    bb.fb1_sb = sb("fb1_sb", [64, 1], F32)
    bb.fw2_sb = sb("fw2_sb", [64, 128], BF16)
    bb.fb2_sb = sb("fb2_sb", [128, 1], F32)
    bb.fw3_sb = sb("fw3_sb", [128, 512], BF16)
    bb.fb3_sb = sb("fb3_sb", [1, 512], BF16)
    bb.swl_sb = sb("swl_sb", [128, 12], F32)
    bb.sbl_sb = sb("sbl_sb", [128, 4], F32)
    bb.io_sb = sb("io_sb", [128, 512], I32)
    bb.pos_sb = sb("pos_sb", [1, 1024], F32)

    bb.fA = lambda b: bb.w1sb[0:64, b * 512:(b + 1) * 512]
    bb.fC = lambda b: bb.w1sb[:, 1024 + b * 1024: 1024 + (b + 1) * 1024]
    bb.fS = lambda b: bb.w1sb[:, 3072 + b * 1024: 3072 + (b + 1) * 1024]
    bb.fP = lambda b: bb.w1sb[:, 5120 + b * 1024: 5120 + (b + 1) * 1024]
    bb.fG = lambda b: bb.w1sb[:, 7168 + b * 1024: 7168 + (b + 1) * 1024]
    bb.fy = lambda b: bb.w1sb[0:64, 9216 + b * 512: 9216 + (b + 1) * 512]
    bb.fT = lambda b: bb.w1sb[:, 10240 + b * 1024: 10240 + (b + 1) * 1024]
    bb.fH = lambda b: bb.hu_sb[:, b * 1024:(b + 1) * 1024]
    bb.vrow = bb.w2sb[:, 0:L + 2]
    bb.vrow2 = bb.w2sb[:, 8256:8256 + L]
    bb.h1_sb = bb.w2sb[0:64, 0:L]
    bb.h2_sb = bb.w2sb[:, 8192:16384]

    ps = lambda n, s, d: bb.ctx.enter_context(nc.psum_tensor(n, s, d))
    bb.psA = ps("psA", [128, 256], F32)
    bb.psB = ps("psB", [128, 256], F32)
    bb.psC = ps("psC", [128, 1024], F32)
    bb.psD = ps("psD", [128, 1024], F32)
    bb.psT1 = ps("psT1", [128, 512], BF16)
    bb.psT2 = ps("psT2", [128, 512], BF16)

    bb.sems = {}
    bb.cnt = {}
    for name in ["c", "x", "u", "ul", "w", "acc", "st", "cc", "vs", "ss", "pls",
                 "peh", "peo", "pet", "se", "ve", "po",
                 "fa", "fh", "fpe", "fve", "fpo", "fst"]:
        bb.sems[name] = bb.ctx.enter_context(nc.semaphore("sem_" + name))
        bb.cnt[name] = 0



def emit_p0(bb, nc):
    casts = [
        (bb.wg_sb.rearrange("p (dc e) -> p dc e", dc=DC), bb.wg.rearrange("(dc p) e -> p dc e", p=128)),
        (bb.ww1_sb[:, :], bb.ww1), (bb.coss_sb[:, :], bb.coss),
        (bb.sins_sb[:, :], bb.sins), (bb.nsins_sb[:, :], bb.nsins),
        (bb.wwA_sb[:, :], bb.wwA), (bb.wwB_sb[:, :], bb.wwB),
        (bb.cosb_sb[:, :], bb.cosb), (bb.nsinb_sb[:, :], bb.nsinb),
        (bb.id_sb[:, :], bb.ident),
        (bb.fw2_sb[:, :], bb.fw2), (bb.fw3_sb[:, :], bb.fw3l),
    ]
    casts += [(bb.bg_sb[:, :], bb.bg), (bb.obr_sb[:, :], bb.obr),
              (bb.bvr_sb[:, :], bb.bvr), (bb.bx2r_sb[:, :], bb.bx2r),
              (bb.fb3_sb[:, :], bb.fb3r)]
    plains = [
        (bb.b1_sb[:, :], bb.b1g),
        (bb.fw1_sb[:, :], bb.fw1),
        (bb.fb1_sb[:, :], bb.fb1g), (bb.fb2_sb[:, :], bb.fb2g),
        (bb.swl_sb[:, :], bb.swl), (bb.sbl_sb[:, :], bb.sbl),
        (bb.twr_sb[:, :], bb.twr), (bb.twi_sb[:, :], bb.twi),
        (bb.twri_sb[:, :], bb.twri), (bb.twii_sb[:, :], bb.twii),
    ]
    n = len(casts) + len(plains)
    with nc.Block() as blk:
        @blk.gpsimd
        def _(g):
            for dst, src in casts:
                g.dma_start(out=dst, in_=src).then_inc(S(bb, "c"), 16)
            for dst, src in plains:
                g.dma_start(out=dst, in_=src).then_inc(S(bb, "c"), 16)
            g.iota(bb.io_sb[:, :], [[1, 512]], channel_multiplier=0,
                   allow_small_or_imprecise_dtypes=True)
            g.wait_ge(S(bb, "c"), 16 * n)

        @blk.vector
        def _(v):
            v.memset(bb.ones_sb[:, :], 1.0)
            v.memset(bb.misc[:, :], 0.0)
    bb.cnt["c"] += 16 * n


def emit_lnpass(bb, nc, src, gating):
    VE = 8 if gating else 6
    SE = 4 if gating else 3
    PE = 5 if gating else 4
    bx, bu = bb.cnt["x"], bb.cnt["u"]
    bve, bse, bpe = bb.cnt["ve"], bb.cnt["se"], bb.cnt["pet"]
    st_done = bb.cnt["st"]
    cvs = [bb.cnt["vs"]]
    css = [bb.cnt["ss"]]
    u_sb = bb.hu_sb[:, 0:TT * D]

    with nc.Block() as blk:
        @blk.sync
        def _(sy):
            sy.wait_ge(S(bb, "st"), st_done)
            for g in range(NTG):
                if g >= 1:
                    sy.wait_ge(S(bb, "ve"), bve + (g - 1) * VE + 2)
                sy.dma_start(
                    out=bb.x_sb.rearrange("p (tt d) -> p tt d", tt=TT),
                    in_=src[g * TG:(g + 1) * TG, :].rearrange("(tt p) d -> p tt d", p=128),
                ).then_inc(S(bb, "x"), 16)
                sy.wait_ge(S(bb, "ve"), bve + g * VE + 6)
                for dc in range(DC):
                    sy.dma_start(
                        out=bb.u_scr[dc * 128:(dc + 1) * 128, g * TG:(g + 1) * TG],
                        in_=bb.uT_sb[:, dc * TG:(dc + 1) * TG],
                    ).then_inc(S(bb, "u"), 16)

        @blk.scalar
        def _(sc):
            for g in range(NTG):
                sc.wait_ge(S(bb, "x"), bx + (g + 1) * 16)
                if g >= 1:
                    sc.wait_ge(S(bb, "pet"), bpe + (g - 1) * PE + 4)
                sc.activation(u_sb[:, 0:D], bb.x_sb[:, 0:D], AF.Square,
                              accum_out=bb.misc[:, MC_SQ:MC_SQ + 1])
                sc.activation(u_sb[:, D:2 * D], bb.x_sb[:, D:2 * D], AF.Square,
                              accum_out=bb.misc[:, MC_SQ + 1:MC_SQ + 2])
                sc.activation(bb.misc[:, 124:125], bb.misc[:, 125:126], AF.Copy)
                sc.activation(bb.misc[:, 124:125], bb.misc[:, 125:126],
                              AF.Copy).then_inc(S(bb, "se"), 1)
                sc.wait_ge(S(bb, "ve"), bve + g * VE + 1)
                sc.activation(bb.misc[:, MC_LNV:MC_LNV + TT],
                              bb.misc[:, MC_VE:MC_VE + TT],
                              AF.Ln).then_inc(S(bb, "se"), 1)
                sc.wait_ge(S(bb, "se"), bse + g * SE + 2)
                sc.activation(bb.misc[:, MC_RSTD:MC_RSTD + TT],
                              bb.misc[:, MC_LNV:MC_LNV + TT], AF.Exp,
                              scale=-0.5).then_inc(S(bb, "se"), 1)
                if gating:
                    sc.wait_ge(S(bb, "ve"), bve + g * VE + 7)
                    sc.activation(bb.misc[:, MC_EXP:MC_EXP + 8],
                                  bb.psA[:, 0:8], AF.Exp,
                                  bias=bb.misc[:, MC_NMX:MC_NMX + 1])
                    sc.activation(bb.misc[:, MC_EXP + 8:MC_EXP + 16],
                                  bb.psA[:, 8:16], AF.Exp,
                                  bias=bb.misc[:, MC_NMX + 1:MC_NMX + 2]
                                  ).then_inc(S(bb, "se"), 1)

        @blk.vector
        def _(v):
            for g in range(NTG):
                v.wait_ge(S(bb, "x"), bx + (g + 1) * 16)
                ins = None
                for tt in range(TT):
                    ins = v.reduce_sum(bb.misc[:, MC_SUM + tt:MC_SUM + tt + 1],
                                       bb.x_sb[:, tt * D:(tt + 1) * D], axis=AX.X)
                ins.then_inc(S(bb, "vs"), 1); cvs[0] += 1
                v.wait_ge(S(bb, "vs"), cvs[0])
                v.tensor_scalar(bb.misc[:, MC_MEAN:MC_MEAN + TT],
                                bb.misc[:, MC_SUM:MC_SUM + TT],
                                float(1.0 / D), None, ALU.mult)
                v.tensor_scalar(bb.misc[:, MC_NEG:MC_NEG + TT],
                                bb.misc[:, MC_SUM:MC_SUM + TT],
                                float(-1.0 / D), None, ALU.mult)
                v.wait_ge(S(bb, "se"), bse + g * SE + 1)
                v.tensor_scalar(bb.misc[:, MC_Q:MC_Q + TT],
                                bb.misc[:, MC_SQ:MC_SQ + TT],
                                float(1.0 / D), float(1e-5), ALU.mult,
                                ALU.add).then_inc(S(bb, "vs"), 1)
                cvs[0] += 1
                v.wait_ge(S(bb, "vs"), cvs[0])
                for tt in range(TT):
                    v.scalar_tensor_tensor(
                        bb.misc[:, MC_VE + tt:MC_VE + tt + 1],
                        bb.misc[:, MC_MEAN + tt:MC_MEAN + tt + 1],
                        bb.misc[:, MC_NEG + tt:MC_NEG + tt + 1],
                        bb.misc[:, MC_Q + tt:MC_Q + tt + 1], ALU.mult, ALU.add)
                v.tensor_copy(bb.misc[:, 126:127], bb.misc[:, 127:128])
                v.tensor_copy(bb.misc[:, 126:127],
                              bb.misc[:, 127:128]).then_inc(S(bb, "ve"), 1)
                v.wait_ge(S(bb, "se"), bse + g * SE + 3)
                for tt in range(TT):
                    v.tensor_scalar(u_sb[:, tt * D:(tt + 1) * D],
                                    bb.x_sb[:, tt * D:(tt + 1) * D],
                                    bb.misc[:, MC_NEG + tt:MC_NEG + tt + 1],
                                    bb.misc[:, MC_RSTD + tt:MC_RSTD + tt + 1],
                                    ALU.add, ALU.mult)
                v.tensor_copy(bb.misc[:, 126:127], bb.misc[:, 127:128])
                v.tensor_copy(bb.misc[:, 126:127],
                              bb.misc[:, 127:128]).then_inc(S(bb, "ve"), 1)
                for q in range(4):
                    v.wait_ge(S(bb, "pet"), bpe + g * PE + q + 1)
                    pst = bb.psT1 if q % 2 == 0 else bb.psT2
                    for j in range(4):
                        idx = q * 4 + j
                        tt2, dc = divmod(idx, DC)
                        v.tensor_copy(
                            bb.uT_sb[:, dc * TG + tt2 * 128: dc * TG + (tt2 + 1) * 128],
                            pst[:, j * 128:(j + 1) * 128])
                    v.tensor_copy(bb.misc[:, 126:127], bb.misc[:, 127:128])
                    v.tensor_copy(bb.misc[:, 126:127],
                                  bb.misc[:, 127:128]).then_inc(S(bb, "ve"), 1)
                if gating:
                    v.wait_ge(S(bb, "pet"), bpe + g * PE + 5)
                    ins = None
                    for tt in range(TT):
                        v.reduce_max(bb.misc[:, MC_MX + tt:MC_MX + tt + 1],
                                     bb.psA[:, tt * 8:(tt + 1) * 8], axis=AX.X)
                        ins = v.reduce_max(bb.misc[:, MC_NMX + tt:MC_NMX + tt + 1],
                                           bb.psA[:, tt * 8:(tt + 1) * 8], axis=AX.X,
                                           negate=True)
                    ins.then_inc(S(bb, "vs"), 1); cvs[0] += 1
                    v.wait_ge(S(bb, "vs"), cvs[0])
                    ins = None
                    for tt in range(TT):
                        ins = v.tensor_scalar(bb.misc[:, MC_MSK + 8 * tt:MC_MSK + 8 * tt + 8],
                                              bb.psA[:, tt * 8:(tt + 1) * 8],
                                              bb.misc[:, MC_MX + tt:MC_MX + tt + 1],
                                              None, ALU.is_equal)
                    ins.then_inc(S(bb, "ve"), 1)
                    v.wait_ge(S(bb, "se"), bse + g * SE + 4)
                    ins = None
                    for tt in range(TT):
                        ins = v.reduce_sum(bb.misc[:, MC_SUME + tt:MC_SUME + tt + 1],
                                           bb.misc[:, MC_EXP + 8 * tt:MC_EXP + 8 * tt + 8],
                                           axis=AX.X)
                    ins.then_inc(S(bb, "vs"), 1); cvs[0] += 1
                    v.wait_ge(S(bb, "vs"), cvs[0])
                    ins = None
                    for tt in range(TT):
                        ins = v.reciprocal(bb.misc[:, MC_WINV + tt:MC_WINV + tt + 1],
                                           bb.misc[:, MC_SUME + tt:MC_SUME + tt + 1])
                    ins.then_inc(S(bb, "vs"), 1); cvs[0] += 1
                    v.wait_ge(S(bb, "vs"), cvs[0])
                    for tt in range(TT):
                        col = (g * TT + tt) * NE
                        v.tensor_scalar(bb.s_sb[:, col:col + NE],
                                        bb.misc[:, MC_MSK + 8 * tt:MC_MSK + 8 * tt + 8],
                                        bb.misc[:, MC_WINV + tt:MC_WINV + tt + 1],
                                        float(0.5), ALU.mult, ALU.mult)
                    v.tensor_copy(bb.misc[:, 126:127], bb.misc[:, 127:128])
                    v.tensor_copy(bb.misc[:, 126:127],
                                  bb.misc[:, 127:128]).then_inc(S(bb, "ve"), 1)

        @blk.tensor
        def _(t):
            for g in range(NTG):
                t.wait_ge(S(bb, "ve"), bve + g * VE + 2)
                if g >= 1:
                    t.wait_ge(S(bb, "u"), bu + g * 128)
                for q in range(4):
                    if q >= 2:
                        t.wait_ge(S(bb, "ve"), bve + g * VE + q + 1)
                    pst = bb.psT1 if q % 2 == 0 else bb.psT2
                    ins = None
                    for j in range(4):
                        idx = q * 4 + j
                        tt2, dc = divmod(idx, DC)
                        ins = t.transpose(pst[:, j * 128:(j + 1) * 128],
                                          u_sb[:, tt2 * D + dc * 128: tt2 * D + (dc + 1) * 128],
                                          bb.id_sb[:, :])
                    ins.then_inc(S(bb, "pet"), 1)
                if gating:
                    ins = None
                    for tt in range(TT):
                        for dc in range(DC):
                            t.ldweights(bb.uT_sb[:, dc * TG + tt * 128: dc * TG + (tt + 1) * 128])
                            t.matmul(bb.psA[:, tt * 8:(tt + 1) * 8],
                                     bb.uT_sb[:, dc * TG + tt * 128: dc * TG + (tt + 1) * 128],
                                     bb.wg_sb[:, dc * NE:(dc + 1) * NE],
                                     start=(dc == 0), stop=False)
                        t.ldweights(bb.ones_sb[:, 0:128])
                        ins = t.matmul(bb.psA[:, tt * 8:(tt + 1) * 8],
                                       bb.ones_sb[:, 0:128], bb.bg_sb[:, :],
                                       start=False, stop=True)
                    ins.then_inc(S(bb, "pet"), 1)

    bb.cnt["x"] += NTG * 16
    bb.cnt["u"] += NTG * 128
    bb.cnt["vs"] = cvs[0]
    bb.cnt["ss"] = css[0]
    bb.cnt["ve"] += NTG * VE
    bb.cnt["se"] += NTG * SE
    bb.cnt["pet"] += NTG * PE


def emit_expert(bb, nc, e):
    bu, bw = bb.cnt["ul"], bb.cnt["w"]
    bh, bo = bb.cnt["peh"], bb.cnt["peo"]
    bs, bv, ba = bb.cnt["se"], bb.cnt["ve"], bb.cnt["acc"]
    u_done = bb.cnt["u"]
    x_done = bb.cnt["x"]
    FJ = ["fa", "fh", "fpe", "fve"]
    GJ = ["fpo", "fst", "vs", "ss"]
    bFJ = [bb.cnt[n] for n in FJ]
    bGJ = [bb.cnt[n] for n in GJ]
    htile = lambda p: bb.hu_sb[:, p * TG:(p + 1) * TG]

    with nc.Block() as blk:
        @blk.sync
        def _(sy):
            sy.wait_ge(S(bb, "u"), u_done)
            for g in range(NTG):
                if g >= 2:
                    sy.wait_ge(S(bb, "peh"), bh + (g - 1) * HC)
                sy.dma_start(
                    out=bb.uT_sb[:, (g % 2) * DC * TG:((g % 2) + 1) * DC * TG].rearrange("p (dc t) -> p dc t", dc=DC),
                    in_=bb.u_scr[:, g * TG:(g + 1) * TG].rearrange(
                        "(dc p) t -> p dc t", p=128),
                ).then_inc(S(bb, "ul"), 16)

        @blk.tensor
        def _(t):
            for g in range(NTG):
                t.wait_ge(S(bb, "ul"), bu + (g + 1) * 16)
                ub = (g % 2) * DC * TG
                for hc in range(HC + 1):
                    if hc < HC:
                        if g == 0:
                            if hc < 8:
                                t.wait_ge(S(bb, FJ[hc // 2]), bFJ[hc // 2] + 16)
                            else:
                                t.wait_ge(S(bb, "w"), bw + 16 * (2 * (hc // 8) - 1))
                        lin = g * HC + hc
                        if lin >= 2:
                            t.wait_ge(S(bb, "se"), bs + lin - 1)
                        psh = bb.psA if hc % 2 == 0 else bb.psB
                        ins = None
                        for dc in range(DC):
                            w1tile = bb.w1sb[:, (hc * DC + dc) * 128:(hc * DC + dc + 1) * 128]
                            t.ldweights(w1tile)
                            ins = t.matmul(psh[:, :], w1tile,
                                           bb.uT_sb[:, ub + dc * TG: ub + (dc + 1) * TG],
                                           start=(dc == 0), stop=(dc == DC - 1))
                        ins.then_inc(S(bb, "peh"), 1)
                    if hc >= 1:
                        hcm = hc - 1
                        if g == 0:
                            if hcm < 8:
                                t.wait_ge(S(bb, GJ[hcm // 2]), bGJ[hcm // 2] + 16)
                            else:
                                t.wait_ge(S(bb, "w"), bw + 16 * (2 * (hcm // 8)))
                        t.wait_ge(S(bb, "se"), bs + g * HC + hcm + 1)
                        if hcm == 0:
                            if g >= 1:
                                t.wait_ge(S(bb, "ve"), bv + g * 2)
                        for ts in range(TT):
                            pso = bb.psC if ts == 0 else bb.psD
                            ht = htile(hcm % 2)[:, ts * 128:(ts + 1) * 128]
                            t.ldweights(ht)
                            for half in range(2):
                                ins = t.matmul(
                                    pso[:, half * 512:(half + 1) * 512], ht,
                                    bb.w2sb[:, hcm * D + half * 512: hcm * D + (half + 1) * 512],
                                    start=(hcm == 0), stop=False)
                if g == 0:
                    t.wait_ge(S(bb, "w"), bw + 112)
                ins = None
                for ts in range(TT):
                    pso = bb.psC if ts == 0 else bb.psD
                    t.ldweights(bb.ones_sb[:, 0:128])
                    for half in range(2):
                        ins = t.matmul(pso[:, half * 512:(half + 1) * 512],
                                       bb.ones_sb[:, 0:128],
                                       bb.b2e_sb[:, half * 512:(half + 1) * 512],
                                       start=False, stop=True)
                    ins.then_inc(S(bb, "peo"), 1)

        @blk.scalar
        def _(sc):
            for g in range(NTG):
                for hc in range(HC):
                    lin = g * HC + hc
                    sc.wait_ge(S(bb, "peh"), bh + lin + 1)
                    psh = bb.psA if hc % 2 == 0 else bb.psB
                    sc.activation(htile(hc % 2), psh[:, :], AF.Silu,
                                  bias=bb.b1_sb[:, e * HC + hc: e * HC + hc + 1]
                                  ).then_inc(S(bb, "se"), 1)

        @blk.vector
        def _(v):
            for g in range(NTG):
                if g >= 2:
                    v.wait_ge(S(bb, "acc"), ba + (g - 1) * 16)
                elif e > 0:
                    v.wait_ge(S(bb, "acc"), ba - (1 - g) * 16)
                for ts in range(TT):
                    v.wait_ge(S(bb, "peo"), bo + g * TT + ts + 1)
                    pso = bb.psC if ts == 0 else bb.psD
                    col = (g * TT + ts) * NE + e
                    v.tensor_scalar(
                        bb.tmp_sb[:, (g % 2) * TT * D + ts * D: (g % 2) * TT * D + (ts + 1) * D],
                        pso[:, :], bb.s_sb[:, col:col + 1], None,
                        ALU.mult).then_inc(S(bb, "ve"), 1)

        @blk.gpsimd
        def _(gp):
            for j in range(4):
                gp.dma_start(
                    out=bb.w1sb[:, j * 2048:(j + 1) * 2048].rearrange(
                        "p (hc r) -> p hc r", hc=2),
                    in_=bb.w1t[e, :, 2 * j:2 * j + 2, :, :].rearrange(
                        "p hc dc q -> p hc (dc q)"),
                ).then_inc(S(bb, FJ[j]), 16)
                gp.dma_start(
                    out=bb.w2sb[:, j * 2 * D:(j + 1) * 2 * D].rearrange(
                        "p (hc d) -> p hc d", hc=2),
                    in_=bb.w2r[e, j * 256:(j + 1) * 256, :].rearrange(
                        "(hc p) d -> p hc d", p=128),
                ).then_inc(S(bb, GJ[j]), 16)
            for k in range(1, 4):
                gp.dma_start(
                    out=bb.w1sb[:, k * 8192:(k + 1) * 8192].rearrange(
                        "p (hc r) -> p hc r", hc=8),
                    in_=bb.w1t[e, :, 8 * k:8 * k + 8, :, :].rearrange(
                        "p hc dc q -> p hc (dc q)"),
                ).then_inc(S(bb, "w"), 16)
                gp.dma_start(
                    out=bb.w2sb[:, k * 8 * D:(k + 1) * 8 * D].rearrange(
                        "p (hc d) -> p hc d", hc=8),
                    in_=bb.w2r[e, k * 1024:(k + 1) * 1024, :].rearrange(
                        "(hc p) d -> p hc d", p=128),
                ).then_inc(S(bb, "w"), 16)
            gp.dma_start(out=bb.b2e_sb[:, :],
                         in_=bb.b2r[:, e * D:(e + 1) * D]).then_inc(S(bb, "w"), 16)
            gp.wait_ge(S(bb, "x"), x_done)
            for g in range(NTG):
                gp.wait_ge(S(bb, "ve"), bv + (g + 1) * TT)
                if e > 0 or g > 0:
                    gp.wait_ge(S(bb, "acc"), ba + g * 16)
                gp.dma_start(
                    out=bb.y_scr[g * TG:(g + 1) * TG, :].rearrange(
                        "(tt p) d -> p tt d", p=128),
                    in_=bb.tmp_sb[:, (g % 2) * TT * D:((g % 2) + 1) * TT * D
                                  ].rearrange("p (tt d) -> p tt d", tt=TT),
                    accum_op=(ALU.bypass if e == 0 else ALU.add),
                ).then_inc(S(bb, "acc"), 16)

    bb.cnt["w"] += 112
    for n in FJ + GJ:
        bb.cnt[n] += 16
    bb.cnt["ul"] += NTG * 16
    bb.cnt["peh"] += NTG * HC
    bb.cnt["peo"] += NTG * TT
    bb.cnt["se"] += NTG * HC
    bb.cnt["ve"] += NTG * TT
    bb.cnt["acc"] += NTG * 16


def emit_postadd(bb, nc, srcA, dst, final):
    bx, bve = bb.cnt["x"], bb.cnt["ve"]
    bst = bb.cnt["st"]
    with nc.Block() as blk:
        @blk.sync
        def _(sy):
            sy.wait_ge(S(bb, "acc"), bb.cnt["acc"])
            sy.wait_ge(S(bb, "st"), bst)
            for g in range(NTG):
                if g >= 1:
                    sy.wait_ge(S(bb, "ve"), bve + g)
                sy.dma_start(
                    out=bb.x_sb.rearrange("p (tt d) -> p tt d", tt=TT),
                    in_=srcA[g * TG:(g + 1) * TG, :].rearrange("(tt p) d -> p tt d", p=128),
                ).then_inc(S(bb, "x"), 16)
                sy.dma_start(
                    out=bb.tmp_sb[:, 0:TT * D].rearrange("p (tt d) -> p tt d", tt=TT),
                    in_=bb.y_scr[g * TG:(g + 1) * TG, :].rearrange("(tt p) d -> p tt d", p=128),
                ).then_inc(S(bb, "x"), 16)
                sy.wait_ge(S(bb, "ve"), bve + g + 1)
                sy.dma_start(
                    out=dst[g * TG:(g + 1) * TG, :].rearrange("(tt p) d -> p tt d", p=128),
                    in_=bb.tmp_sb[:, TT * D:2 * TT * D].rearrange("p (tt d) -> p tt d", tt=TT),
                ).then_inc(S(bb, "st"), 16)
            if final:
                sy.wait_ge(S(bb, "st"), bst + NTG * 16)

        @blk.vector
        def _(v):
            for g in range(NTG):
                v.wait_ge(S(bb, "x"), bx + (g + 1) * 32)
                if g >= 1:
                    v.wait_ge(S(bb, "st"), bst + g * 16)
                v.tensor_add(bb.tmp_sb[:, TT * D:2 * TT * D],
                             bb.x_sb[:, :], bb.tmp_sb[:, 0:TT * D]).then_inc(S(bb, "ve"), 1)

    bb.cnt["x"] += NTG * 32
    bb.cnt["ve"] += NTG
    bb.cnt["st"] += NTG * 16


def emit_projpass(bb, nc):
    bu, bw = bb.cnt["ul"], bb.cnt["w"]
    bpe, bse, bst = bb.cnt["pet"], bb.cnt["se"], bb.cnt["st"]
    u_done = bb.cnt["u"]
    outs = [(bb.vin_cc, None, 0), (bb.xb2_scr, None, 1)]

    with nc.Block() as blk:
        @blk.gpsimd
        def _(g):
            g.dma_start(out=bb.w1sb[:, 0:DC * DC * 128],
                        in_=bb.wvt).then_inc(S(bb, "w"), 16)
            g.dma_start(out=bb.w1sb[:, DC * DC * 128:2 * DC * DC * 128],
                        in_=bb.wx2t).then_inc(S(bb, "w"), 16)

        @blk.sync
        def _(sy):
            sy.wait_ge(S(bb, "u"), u_done)
            for g in range(NTG):
                if g >= 2:
                    sy.wait_ge(S(bb, "pet"), bpe + (g - 1) * 16)
                sy.dma_start(
                    out=bb.uT_sb[:, (g % 2) * DC * TG:((g % 2) + 1) * DC * TG].rearrange("p (dc t) -> p dc t", dc=DC),
                    in_=bb.u_scr[:, g * TG:(g + 1) * TG].rearrange(
                        "(dc p) t -> p dc t", p=128),
                ).then_inc(S(bb, "ul"), 16)
                for p_i, (dstD, _b, _o) in enumerate(outs):
                    for cc in range(DC):
                        k = p_i * DC + cc
                        sy.wait_ge(S(bb, "se"), bse + g * 16 + k + 1)
                        sy.dma_start(
                            out=dstD[cc * 128:(cc + 1) * 128, g * TG:(g + 1) * TG],
                            in_=bb.hu_sb[:, (k % 2) * TG + 2 * TG:(k % 2) * TG + 2 * TG + TG],
                        ).then_inc(S(bb, "st"), 16)

        @blk.tensor
        def _(t):
            t.wait_ge(S(bb, "w"), bw + 32)
            for g in range(NTG):
                t.wait_ge(S(bb, "ul"), bu + (g + 1) * 16)
                ub = (g % 2) * DC * TG
                for p_i in range(2):
                    for cc in range(DC):
                        k = p_i * DC + cc
                        if g * 16 + k >= 2:
                            t.wait_ge(S(bb, "se"), bse + g * 16 + k - 1)
                        psp = bb.psA if k % 2 == 0 else bb.psB
                        for dc in range(DC):
                            wt = bb.w1sb[:, (p_i * 64 + cc * DC + dc) * 128:
                                         (p_i * 64 + cc * DC + dc + 1) * 128]
                            t.ldweights(wt)
                            t.matmul(psp[:, :], wt,
                                     bb.uT_sb[:, ub + dc * TG: ub + (dc + 1) * TG],
                                     start=(dc == 0), stop=False)
                        br = bb.bvr_sb if p_i == 0 else bb.bx2r_sb
                        t.ldweights(br[:, cc * 128:(cc + 1) * 128])
                        t.matmul(psp[:, :], br[:, cc * 128:(cc + 1) * 128],
                                 bb.ones_sb[:, 0:TG], start=False,
                                 stop=True).then_inc(S(bb, "pet"), 1)

        @blk.scalar
        def _(sc):
            for g in range(NTG):
                for p_i in range(2):
                    for cc in range(DC):
                        k = p_i * DC + cc
                        sc.wait_ge(S(bb, "pet"), bpe + g * 16 + k + 1)
                        if g * 16 + k >= 2:
                            sc.wait_ge(S(bb, "st"), bst + (g * 16 + k - 1) * 16)
                        psp = bb.psA if k % 2 == 0 else bb.psB
                        sc.activation(
                            bb.hu_sb[:, (k % 2) * TG + 2 * TG:(k % 2) * TG + 2 * TG + TG],
                            psp[:, :], AF.Copy).then_inc(S(bb, "se"), 1)

    bb.cnt["w"] += 32
    bb.cnt["ul"] += NTG * 16
    bb.cnt["pet"] += NTG * 16
    bb.cnt["se"] += NTG * 16
    bb.cnt["st"] += NTG * 16 * 16



RG_PAIRS = [[0, 1], [2, 3], [4, 5], [6, 7]]


def emit_hysetup(bb, nc):
    st_done = bb.cnt["st"]
    bpo = bb.cnt["po"]
    bph = bb.cnt["peh"]
    bfh = bb.cnt["fh"]
    bfa = bb.cnt["fa"]
    bfst = bb.cnt["fst"]
    bfpe = bb.cnt["fpe"]
    bfve_pos = bb.cnt["fve"]
    bfve = bb.cnt["fve"] + 16
    cvs = [bb.cnt["vs"]]

    with nc.Block() as blk:
        @blk.tensor
        def _(t):
            for c in range(16):
                if c >= 2:
                    t.wait_ge(S(bb, "po"), bpo + c - 1)
                t.wait_ge(S(bb, "fve"), bfve_pos + c + 1)
                psx = bb.psC if c % 2 == 0 else bb.psD
                t.matmul(psx[0:64, 0:512], bb.fw1_sb[:, :],
                         bb.pos_sb[0:1, (c % 2) * 512:(c % 2) * 512 + 512],
                         start=True, stop=True).then_inc(S(bb, "peh"), 1)
            for c in range(16):
                t.wait_ge(S(bb, "po"), bpo + c + 1)
                t.wait_ge(S(bb, "po"), bpo + (16 + c - 1 if c >= 2 else 15 + c))
                psx = bb.psC if c % 2 == 0 else bb.psD
                t.ldweights(bb.fw2_sb[:, :])
                t.matmul(psx[:, 0:512], bb.fw2_sb[:, :],
                         bb.h1_sb[:, c * 512:(c + 1) * 512],
                         start=True, stop=True).then_inc(S(bb, "peh"), 1)
            for ccl in range(4):
                for c in range(16):
                    k = ccl * 16 + c
                    t.wait_ge(S(bb, "po"), bpo + 16 + c + 1)
                    t.wait_ge(S(bb, "po"), bpo + (32 + k - 1 if k >= 2 else 31 + k))
                    psx = bb.psC if k % 2 == 0 else bb.psD
                    t.ldweights(bb.fw3_sb[:, ccl * 128:(ccl + 1) * 128])
                    t.matmul(psx[:, 0:512],
                             bb.fw3_sb[:, ccl * 128:(ccl + 1) * 128],
                             bb.h2_sb[:, c * 512:(c + 1) * 512],
                             start=True, stop=False)
                    t.ldweights(bb.fb3_sb[:, ccl * 128:(ccl + 1) * 128])
                    t.matmul(psx[:, 0:512], bb.fb3_sb[:, ccl * 128:(ccl + 1) * 128],
                             bb.ones_sb[:, :], start=False,
                             stop=True).then_inc(S(bb, "peh"), 1)
            for g2 in range(NFG):
                t.wait_ge(S(bb, "fa"), bfa + (g2 + 1) * 16)
                if g2 == 0:
                    t.wait_ge(S(bb, "po"), bpo + 96)
                if g2 >= 1:
                    t.wait_ge(S(bb, "fve"), bfve + 2 * (g2 - 1) + 1)
                for s in range(FG):
                    at = bb.fA(g2 % 2)[:, s * 128:(s + 1) * 128]
                    t.ldweights(at)
                    t.matmul(bb.psC[:, s * 128:(s + 1) * 128], at,
                             bb.ww1_sb[:, 0:128], start=True, stop=True)
                    ins = t.matmul(bb.psC[:, 512 + s * 128:512 + (s + 1) * 128], at,
                                   bb.ww1_sb[:, 128:256], start=True, stop=True)
                ins.then_inc(S(bb, "fpe"), 1)
                t.wait_ge(S(bb, "fve"), bfve + 2 * g2 + 1)
                fCb = bb.fC(g2 % 2)
                rhs3 = fCb.rearrange("p (t s q) -> p s t q", t=2, s=FG, q=128)
                ins = None
                for s in range(FG):
                    t.ldweights(bb.coss_sb[:, :])
                    t.matmul(bb.psD[:, s * 256:(s + 1) * 256], bb.coss_sb[:, :],
                             rhs3[:, s, :, :], start=True, stop=False)
                    t.ldweights(bb.sins_sb[:, :])
                    t.matmul(bb.psD[:, s * 256:s * 256 + 128], bb.sins_sb[:, :],
                             fCb[:, 512 + s * 128:512 + (s + 1) * 128],
                             start=False, stop=False)
                    t.ldweights(bb.nsins_sb[:, :])
                    ins = t.matmul(bb.psD[:, s * 256 + 128:s * 256 + 256],
                                   bb.nsins_sb[:, :],
                                   fCb[:, s * 128:(s + 1) * 128],
                                   start=False, stop=True)
                ins.then_inc(S(bb, "fpe"), 1)

        @blk.scalar
        def _(sc):
            for c in range(16):
                sc.wait_ge(S(bb, "peh"), bph + c + 1)
                psx = bb.psC if c % 2 == 0 else bb.psD
                sc.activation(bb.h1_sb[:, c * 512:(c + 1) * 512], psx[0:64, 0:512],
                              AF.Gelu, bias=bb.fb1_sb[:, 0:1]).then_inc(S(bb, "po"), 1)
            for c in range(16):
                sc.wait_ge(S(bb, "peh"), bph + 16 + c + 1)
                psx = bb.psC if c % 2 == 0 else bb.psD
                sc.activation(bb.h2_sb[:, c * 512:(c + 1) * 512], psx[:, 0:512],
                              AF.Gelu, bias=bb.fb2_sb[:, 0:1]).then_inc(S(bb, "po"), 1)
            for ccl in range(4):
                for c in range(16):
                    k = ccl * 16 + c
                    sc.wait_ge(S(bb, "peh"), bph + 32 + k + 1)
                    if k >= 2:
                        sc.wait_ge(S(bb, "fh"), bfh + (k - 1) * 16)
                    psx = bb.psC if k % 2 == 0 else bb.psD
                    sc.activation(bb.hu_sb[:, (k % 2) * 512:(k % 2) * 512 + 512],
                                  psx[:, 0:512], AF.Copy).then_inc(S(bb, "po"), 1)

        @blk.vector
        def _(v):
            for c in range(16):
                if c >= 2:
                    v.wait_ge(S(bb, "peh"), bph + c - 1)
                v.tensor_scalar(bb.pos_sb[0:1, (c % 2) * 512:(c % 2) * 512 + 512],
                                bb.io_sb[0:1, :], float(1.0 / 8191.0),
                                float(c * 512 / 8191.0), ALU.mult,
                                ALU.add).then_inc(S(bb, "fve"), 1)
            for g2 in range(NFG):
                v.wait_ge(S(bb, "fpe"), bfpe + 2 * g2 + 1)
                if g2 >= 2:
                    v.wait_ge(S(bb, "fpe"), bfpe + 2 * (g2 - 2) + 2)
                t0, t1 = bb.tmp_sb[:, 0:512], bb.tmp_sb[:, 512:1024]
                t2, t3 = bb.tmp_sb[:, 1024:1536], bb.tmp_sb[:, 1536:2048]
                fCb = bb.fC(g2 % 2)
                v.tensor_mul(t0, bb.psC[:, 0:512], bb.twr_sb[:, :])
                v.tensor_mul(t1, bb.psC[:, 512:1024], bb.twi_sb[:, :])
                v.tensor_mul(t2, bb.psC[:, 0:512], bb.twi_sb[:, :])
                v.tensor_mul(t3, bb.psC[:, 512:1024], bb.twr_sb[:, :]
                             ).then_inc(S(bb, "vs"), 1)
                cvs[0] += 1
                v.wait_ge(S(bb, "vs"), cvs[0])
                v.tensor_sub(fCb[:, 0:512], t0, t1)
                v.tensor_add(fCb[:, 512:1024], t2, t3)
                v.tensor_copy(bb.misc[:, 126:127], bb.misc[:, 127:128])
                v.tensor_copy(bb.misc[:, 126:127],
                              bb.misc[:, 127:128]).then_inc(S(bb, "fve"), 1)
                v.wait_ge(S(bb, "fpe"), bfpe + 2 * g2 + 2)
                if g2 >= 2:
                    v.wait_ge(S(bb, "fst"), bfst + (g2 - 1) * 16)
                v.tensor_copy(bb.fS(g2 % 2), bb.psD[:, :]).then_inc(S(bb, "fve"), 1)

        @blk.sync
        def _(sy):
            for ccl in range(4):
                for c in range(16):
                    k = ccl * 16 + c
                    sy.wait_ge(S(bb, "po"), bpo + 32 + k + 1)
                    sy.dma_start(
                        out=bb.filt_scr[ccl * 128:(ccl + 1) * 128, c * 512:(c + 1) * 512],
                        in_=bb.hu_sb[:, (k % 2) * 512:(k % 2) * 512 + 512],
                    ).then_inc(S(bb, "fh"), 16)
            for g2 in range(NFG):
                ccl = (4 * g2) // 128
                sy.wait_ge(S(bb, "fh"), bfh + 256 * (ccl + 1))
                if g2 >= 2:
                    sy.wait_ge(S(bb, "fpe"), bfpe + 2 * (g2 - 2) + 1)
                sy.dma_start(
                    out=bb.fA(g2 % 2).rearrange("p (s n2) -> p s n2", s=FG),
                    in_=bb.filt_scr[4 * g2:4 * g2 + 4, :].rearrange(
                        "s (n1 n2) -> n1 s n2", n1=64),
                ).then_inc(S(bb, "fa"), 16)
                sy.wait_ge(S(bb, "fve"), bfve + 2 * g2 + 2)
                sy.dma_start(
                    out=bb.hspec_scr[4 * g2:4 * g2 + 4, :].rearrange(
                        "s (p q) -> p s q", p=128),
                    in_=bb.fS(g2 % 2).rearrange("p (s q) -> p s q", s=FG),
                ).then_inc(S(bb, "fst"), 16)

    bb.cnt["peh"] += 96
    bb.cnt["po"] += 96
    bb.cnt["fh"] += 64 * 16
    bb.cnt["fa"] += NFG * 16
    bb.cnt["fst"] += NFG * 16
    bb.cnt["fpe"] += NFG * 2
    bb.cnt["vs"] = cvs[0]
    bb.cnt["fve"] += NFG * 2 + 16


def emit_ccv(bb, nc):
    st_done = bb.cnt["st"]
    with nc.Block() as blk:
        @blk.gpsimd
        def _(g):
            g.wait_ge(S(bb, "st"), st_done)
            g.collective_compute(
                "AllGather", ALU.bypass, replica_groups=RG_PAIRS,
                ins=[bb.vin_cc.opt()], outs=[bb.vout_cc.opt()],
            ).then_inc(S(bb, "cc"), 1)
            g.wait_ge(S(bb, "cc"), 1)
    bb.cnt["cc"] += 1


def emit_shortconv(bb, nc):
    bx, bve, bst = bb.cnt["x"], bb.cnt["ve"], bb.cnt["st"]
    cvs = [bb.cnt["vs"]]
    vre = bb.vout_cc.rearrange("(s c) t -> c s t", s=2)
    with nc.Block() as blk:
        @blk.sync
        def _(sy):
            sy.wait_ge(S(bb, "cc"), 1)
            pid = nc.partition_id(engines=[ET.SP])
            par = pid % 2
            for cc in range(4):
                if cc >= 1:
                    sy.wait_ge(S(bb, "ve"), bve + cc)
                with sy.If(par < 1):
                    sy.dma_start(out=bb.vrow[:, 1:L + 1].rearrange("p (s t) -> p s t", s=2),
                                 in_=vre[cc * 128:(cc + 1) * 128, :, :]).then_inc(S(bb, "x"), 16)
                with sy.Else():
                    sy.dma_start(out=bb.vrow[:, 1:L + 1].rearrange("p (s t) -> p s t", s=2),
                                 in_=vre[512 + cc * 128:512 + (cc + 1) * 128, :, :]
                                 ).then_inc(S(bb, "x"), 16)
                sy.wait_ge(S(bb, "ve"), bve + cc + 1)
                sy.dma_start(out=bb.vpr_scr[cc * 128:(cc + 1) * 128, :],
                             in_=bb.vrow2[:, :]).then_inc(S(bb, "st"), 16)

        @blk.vector
        def _(v):
            v.memset(bb.vrow[:, 0:1], 0.0)
            v.memset(bb.vrow[:, L + 1:L + 2], 0.0)
            for cc in range(4):
                v.wait_ge(S(bb, "x"), bx + (cc + 1) * 16)
                if cc >= 1:
                    v.wait_ge(S(bb, "st"), bst + cc * 16)
                v.tensor_scalar(bb.vrow2[:, :], bb.vrow[:, 0:L],
                                bb.swl_sb[:, cc * 3:cc * 3 + 1],
                                bb.sbl_sb[:, cc:cc + 1], ALU.mult,
                                ALU.add).then_inc(S(bb, "vs"), 1)
                cvs[0] += 1
                v.wait_ge(S(bb, "vs"), cvs[0])
                v.scalar_tensor_tensor(bb.vrow2[:, :], bb.vrow[:, 1:L + 1],
                                       bb.swl_sb[:, cc * 3 + 1:cc * 3 + 2],
                                       bb.vrow2[:, :], ALU.mult,
                                       ALU.add).then_inc(S(bb, "vs"), 1)
                cvs[0] += 1
                v.wait_ge(S(bb, "vs"), cvs[0])
                v.scalar_tensor_tensor(bb.vrow2[:, :], bb.vrow[:, 2:L + 2],
                                       bb.swl_sb[:, cc * 3 + 2:cc * 3 + 3],
                                       bb.vrow2[:, :], ALU.mult, ALU.add
                                       ).then_inc(S(bb, "ve"), 1)

    bb.cnt["x"] += 4 * 16
    bb.cnt["ve"] += 4
    bb.cnt["vs"] = cvs[0]
    bb.cnt["st"] += 4 * 16


def emit_vfft(bb, nc):
    NIT = NFG + 2
    sched = {"pe": [], "ve": [], "po": [], "A": [], "H": [], "Y": []}
    for g in range(NIT):
        if g < NFG:
            sched["A"].append(("A", g)); sched["H"].append(("H", g))
            sched["pe"].append(("s1", g)); sched["pe"].append(("s2", g))
            sched["ve"].append(("tw", g)); sched["ve"].append(("evacV", g))
            sched["po"].append(("ptw", g))
        if 1 <= g <= NFG:
            sched["pe"].append(("iA", g - 1))
            sched["ve"].append(("itw", g - 1))
        if 2 <= g <= NFG + 1:
            sched["pe"].append(("iB", g - 2))
            sched["ve"].append(("yevac", g - 2))
            sched["Y"].append(("Y", g - 2))
    idx = {}
    for eng in sched:
        for i, key in enumerate(sched[eng]):
            idx[(eng,) + key] = i + 1
    ve_end = [0] * (NIT + 1)
    cnt_v = 0
    for g in range(NIT):
        n_in_it = (2 if g < NFG else 0) + (1 if 1 <= g <= NFG else 0) + (1 if 2 <= g <= NFG + 1 else 0)
        cnt_v += n_in_it
        ve_end[g + 1] = cnt_v

    bfa, bfh, bfst = bb.cnt["fa"], bb.cnt["fh"], bb.cnt["fst"]
    bfpe, bfve, bfpo = bb.cnt["fpe"], bb.cnt["fve"], bb.cnt["fpo"]
    st_short = bb.cnt["st"]

    def C(eng, st, g):
        return idx[(eng, st, g)]
    cvs = [bb.cnt["vs"]]
    cpls = [bb.cnt["pls"]]

    with nc.Block() as blk:
        @blk.sync
        def _(sy):
            sy.wait_ge(S(bb, "st"), st_short)
            sy.wait_ge(S(bb, "fst"), bfst)
            for g in range(NIT):
                if g < NFG:
                    if g >= 2:
                        sy.wait_ge(S(bb, "fpe"), bfpe + C("pe", "s1", g - 2))
                    sy.dma_start(
                        out=bb.fA(g % 2).rearrange("p (s n2) -> p s n2", s=FG),
                        in_=bb.vpr_scr[4 * g:4 * g + 4, :].rearrange(
                            "s (n1 n2) -> n1 s n2", n1=64),
                    ).then_inc(S(bb, "fa"), 16)
                    if g >= 2:
                        sy.wait_ge(S(bb, "fpo"), bfpo + C("po", "ptw", g - 2))
                    sy.dma_start(
                        out=bb.fH(g % 2).rearrange("p (s q) -> p s q", s=FG),
                        in_=bb.hspec_scr[4 * g:4 * g + 4, :].rearrange(
                            "s (p q) -> p s q", p=128),
                    ).then_inc(S(bb, "fh"), 16)
                if 2 <= g:
                    j = g - 2
                    if j < NFG:
                        sy.wait_ge(S(bb, "fve"), bfve + C("ve", "yevac", j))
                        sy.dma_start(
                            out=bb.cin_cc[4 * j * 64:(4 * j + 4) * 64, :].rearrange(
                                "(s n1) n2 -> n1 s n2", s=4),
                            in_=bb.fy(j % 2).rearrange("p (s q) -> p s q", s=FG),
                        ).then_inc(S(bb, "fst"), 16)

        @blk.tensor
        def _(t):
            for g in range(NIT):
                if g < NFG:
                    t.wait_ge(S(bb, "fa"), bfa + 16 * (g + 1))
                    if g >= 2:
                        t.wait_ge(S(bb, "fve"), bfve + C("ve", "itw", g - 2))
                    ins = None
                    for s in range(FG):
                        at = bb.fA(g % 2)[:, s * 128:(s + 1) * 128]
                        t.ldweights(at)
                        t.matmul(bb.psC[:, s * 128:(s + 1) * 128], at,
                                 bb.ww1_sb[:, 0:128], start=True, stop=True)
                        ins = t.matmul(bb.psC[:, 512 + s * 128:512 + (s + 1) * 128], at,
                                       bb.ww1_sb[:, 128:256], start=True, stop=True)
                    ins.then_inc(S(bb, "fpe"), 1)
                    t.wait_ge(S(bb, "fve"), bfve + C("ve", "tw", g))
                    fCb = bb.fC(g % 2)
                    rhs3 = fCb.rearrange("p (t s q) -> p s t q", t=2, s=FG, q=128)
                    ins = None
                    for s in range(FG):
                        t.ldweights(bb.coss_sb[:, :])
                        t.matmul(bb.psD[:, s * 256:(s + 1) * 256], bb.coss_sb[:, :],
                                 rhs3[:, s, :, :], start=True, stop=False)
                        t.ldweights(bb.sins_sb[:, :])
                        t.matmul(bb.psD[:, s * 256:s * 256 + 128], bb.sins_sb[:, :],
                                 fCb[:, 512 + s * 128:512 + (s + 1) * 128],
                                 start=False, stop=False)
                        t.ldweights(bb.nsins_sb[:, :])
                        ins = t.matmul(bb.psD[:, s * 256 + 128:s * 256 + 256],
                                       bb.nsins_sb[:, :],
                                       fCb[:, s * 128:(s + 1) * 128],
                                       start=False, stop=True)
                    ins.then_inc(S(bb, "fpe"), 1)
                if 1 <= g <= NFG:
                    j = g - 1
                    t.wait_ge(S(bb, "fpo"), bfpo + C("po", "ptw", j))
                    fPb = bb.fP(j % 2)
                    ins = None
                    for s in range(FG):
                        pr = fPb[:, s * 256:s * 256 + 128]
                        pi = fPb[:, s * 256 + 128:s * 256 + 256]
                        t.ldweights(pr)
                        t.matmul(bb.psC[:, s * 128:(s + 1) * 128], pr,
                                 bb.wwA_sb[:, 0:128], start=True, stop=False)
                        t.matmul(bb.psC[:, 512 + s * 128:512 + (s + 1) * 128], pr,
                                 bb.wwA_sb[:, 128:256], start=True, stop=False)
                        t.ldweights(pi)
                        t.matmul(bb.psC[:, s * 128:(s + 1) * 128], pi,
                                 bb.wwB_sb[:, 0:128], start=False, stop=True)
                        ins = t.matmul(bb.psC[:, 512 + s * 128:512 + (s + 1) * 128], pi,
                                       bb.wwB_sb[:, 128:256], start=False, stop=True)
                    ins.then_inc(S(bb, "fpe"), 1)
                if 2 <= g <= NFG + 1:
                    j = g - 2
                    t.wait_ge(S(bb, "fve"), bfve + C("ve", "itw", j))
                    fGb = bb.fG(j % 2)
                    slots = [bb.psA[0:64, 0:128], bb.psA[0:64, 128:256],
                             bb.psB[0:64, 0:128], bb.psB[0:64, 128:256]]
                    ins = None
                    for s in range(FG):
                        t.ldweights(bb.cosb_sb[:, :])
                        t.matmul(slots[s], bb.cosb_sb[:, :],
                                 fGb[:, s * 128:(s + 1) * 128], start=True, stop=False)
                        t.ldweights(bb.nsinb_sb[:, :])
                        ins = t.matmul(slots[s], bb.nsinb_sb[:, :],
                                       fGb[:, 512 + s * 128:512 + (s + 1) * 128],
                                       start=False, stop=True)
                    ins.then_inc(S(bb, "fpe"), 1)

        @blk.vector
        def _(v):
            t0, t1 = bb.tmp_sb[:, 0:512], bb.tmp_sb[:, 512:1024]
            t2, t3 = bb.tmp_sb[:, 1024:1536], bb.tmp_sb[:, 1536:2048]
            for g in range(NIT):
                if g < NFG:
                    v.wait_ge(S(bb, "fpe"), bfpe + C("pe", "s1", g))
                    if g >= 2:
                        v.wait_ge(S(bb, "fpe"), bfpe + C("pe", "s2", g - 2))
                    fCb = bb.fC(g % 2)
                    v.tensor_mul(t0, bb.psC[:, 0:512], bb.twr_sb[:, :])
                    v.tensor_mul(t1, bb.psC[:, 512:1024], bb.twi_sb[:, :])
                    v.tensor_mul(t2, bb.psC[:, 0:512], bb.twi_sb[:, :])
                    v.tensor_mul(t3, bb.psC[:, 512:1024], bb.twr_sb[:, :]
                                 ).then_inc(S(bb, "vs"), 1)
                    cvs[0] += 1
                    v.wait_ge(S(bb, "vs"), cvs[0])
                    v.tensor_sub(fCb[:, 0:512], t0, t1)
                    v.tensor_add(fCb[:, 512:1024], t2, t3)
                    v.tensor_copy(bb.misc[:, 126:127], bb.misc[:, 127:128])
                    v.tensor_copy(bb.misc[:, 126:127],
                                  bb.misc[:, 127:128]).then_inc(S(bb, "fve"), 1)
                    v.wait_ge(S(bb, "fpe"), bfpe + C("pe", "s2", g))
                    if g >= 2:
                        v.wait_ge(S(bb, "fpo"), bfpo + C("po", "ptw", g - 2))
                    v.tensor_copy(bb.fS(g % 2), bb.psD[:, :]).then_inc(S(bb, "fve"), 1)
                if 1 <= g <= NFG:
                    j = g - 1
                    v.wait_ge(S(bb, "fpe"), bfpe + C("pe", "iA", j))
                    if j >= 2:
                        v.wait_ge(S(bb, "fpe"), bfpe + C("pe", "iB", j - 2))
                    fGb = bb.fG(j % 2)
                    v.tensor_mul(t0, bb.psC[:, 0:512], bb.twri_sb[:, :])
                    v.tensor_mul(t1, bb.psC[:, 512:1024], bb.twii_sb[:, :])
                    v.tensor_mul(t2, bb.psC[:, 0:512], bb.twii_sb[:, :])
                    v.tensor_mul(t3, bb.psC[:, 512:1024], bb.twri_sb[:, :]
                                 ).then_inc(S(bb, "vs"), 1)
                    cvs[0] += 1
                    v.wait_ge(S(bb, "vs"), cvs[0])
                    v.tensor_sub(fGb[:, 0:512], t0, t1)
                    v.tensor_add(fGb[:, 512:1024], t2, t3)
                    v.tensor_copy(bb.misc[:, 126:127], bb.misc[:, 127:128])
                    v.tensor_copy(bb.misc[:, 126:127],
                                  bb.misc[:, 127:128]).then_inc(S(bb, "fve"), 1)
                if 2 <= g <= NFG + 1:
                    j = g - 2
                    v.wait_ge(S(bb, "fpe"), bfpe + C("pe", "iB", j))
                    if j >= 2:
                        v.wait_ge(S(bb, "fst"), bfst + 16 * (j - 1))
                    v.tensor_copy(bb.fy(j % 2)[:, 0:256], bb.psA[0:64, :])
                    v.tensor_copy(bb.fy(j % 2)[:, 256:512], bb.psB[0:64, :])
                    v.tensor_copy(bb.misc[:, 126:127], bb.misc[:, 127:128])
                    v.tensor_copy(bb.misc[:, 126:127],
                                  bb.misc[:, 127:128]).then_inc(S(bb, "fve"), 1)

        @blk.gpsimd
        def _(p):
            s0, s1_ = bb.tmp_sb[:, 2048:2560], bb.tmp_sb[:, 2560:3072]
            s2_, s3_ = bb.tmp_sb[:, 3072:3584], bb.tmp_sb[:, 3584:4096]
            for g in range(NFG):
                p.wait_ge(S(bb, "fve"), bfve + C("ve", "evacV", g))
                p.wait_ge(S(bb, "fh"), bfh + 16 * (g + 1))
                if g >= 2:
                    p.wait_ge(S(bb, "fpe"), bfpe + C("pe", "iA", g - 2))
                fSb, fHb, fPb = bb.fS(g % 2), bb.fH(g % 2), bb.fP(g % 2)
                vr = fSb.rearrange("p (s t q) -> p t s q", s=FG, t=2, q=128)
                hr = fHb.rearrange("p (s t q) -> p t s q", s=FG, t=2, q=128)
                pr = fPb.rearrange("p (s t q) -> p t s q", s=FG, t=2, q=128)
                s0v = s0.rearrange("p (s q) -> p s q", s=FG)
                s1v = s1_.rearrange("p (s q) -> p s q", s=FG)
                s2v = s2_.rearrange("p (s q) -> p s q", s=FG)
                s3v = s3_.rearrange("p (s q) -> p s q", s=FG)
                p.tensor_mul(s0v, vr[:, 0, :, :], hr[:, 0, :, :])
                p.tensor_mul(s1v, vr[:, 1, :, :], hr[:, 1, :, :])
                p.tensor_mul(s2v, vr[:, 0, :, :], hr[:, 1, :, :])
                p.tensor_mul(s3v, vr[:, 1, :, :], hr[:, 0, :, :]
                             ).then_inc(S(bb, "pls"), 1)
                cpls[0] += 1
                p.wait_ge(S(bb, "pls"), cpls[0])
                p.tensor_sub(pr[:, 0, :, :], s0v, s1v)
                p.tensor_add(pr[:, 1, :, :], s2v, s3v).then_inc(S(bb, "fpo"), 1)

    bb.cnt["fa"] += NFG * 16
    bb.cnt["fh"] += NFG * 16
    bb.cnt["fst"] += NFG * 16
    bb.cnt["fpe"] += len(sched["pe"])
    bb.cnt["fve"] += len(sched["ve"])
    bb.cnt["vs"] = cvs[0]
    bb.cnt["pls"] = cpls[0]
    bb.cnt["fpo"] += NFG


def emit_ccconv(bb, nc):
    fst_done = bb.cnt["fst"]
    with nc.Block() as blk:
        @blk.gpsimd
        def _(g):
            g.wait_ge(S(bb, "fst"), fst_done)
            g.collective_compute(
                "AllGather", ALU.bypass, replica_groups=RG_PAIRS,
                ins=[bb.cin_cc.opt()], outs=[bb.cout_cc.opt()],
            ).then_inc(S(bb, "cc"), 1)
            g.wait_ge(S(bb, "cc"), 2)
    bb.cnt["cc"] += 1


def emit_hyfinish(bb, nc, dst, final):
    bx, bve, bst = bb.cnt["x"], bb.cnt["ve"], bb.cnt["peo"]
    bso = bb.cnt["st"]
    bw = bb.cnt["w"]
    st_prev = bb.cnt["st"]
    cre = bb.cout_cc.rearrange("(c n1) n2 -> c n1 n2", n1=64)

    with nc.Block() as blk:
        @blk.gpsimd
        def _(g):
            g.dma_start(out=bb.w2sb[:, 0:DC * D].rearrange("p (cc d) -> p cc d", cc=DC),
                        in_=bb.owr.rearrange("(cc p) d -> p cc d", p=128)
                        ).then_inc(S(bb, "w"), 16)

        @blk.sync
        def _(sy):
            sy.wait_ge(S(bb, "cc"), 2)
            sy.wait_ge(S(bb, "st"), st_prev)
            pid = nc.partition_id(engines=[ET.SP])
            par = pid % 2
            for g in range(NTG):
                if g >= 1:
                    sy.wait_ge(S(bb, "ve"), bve + g * 3)
                sy.dma_start(
                    out=bb.x_sb.rearrange("p (tt d) -> p tt d", tt=TT),
                    in_=bb.x1_scr[g * TG:(g + 1) * TG, :].rearrange(
                        "(tt p) d -> p tt d", p=128),
                ).then_inc(S(bb, "x"), 16)
                sy.dma_start(
                    out=bb.uT_sb[:, 0:2048].rearrange("p (cc t) -> p cc t", cc=DC),
                    in_=bb.xb2_scr[:, g * TG:(g + 1) * TG].rearrange(
                        "(cc p) t -> p cc t", p=128),
                ).then_inc(S(bb, "x"), 16)
                with sy.If(par < 1):
                    for cc in range(DC):
                        sy.dma_start(
                            out=bb.uT_sb[:, 2048 + cc * 256:2048 + (cc + 1) * 256],
                            in_=cre[cc * 128:(cc + 1) * 128, 2 * g:2 * g + 2, :].rearrange(
                                "c tt q -> c (tt q)"),
                        ).then_inc(S(bb, "x"), 16)
                with sy.Else():
                    for cc in range(DC):
                        sy.dma_start(
                            out=bb.uT_sb[:, 2048 + cc * 256:2048 + (cc + 1) * 256],
                            in_=cre[cc * 128:(cc + 1) * 128, 32 + 2 * g:32 + 2 * g + 2, :
                                    ].rearrange("c tt q -> c (tt q)"),
                        ).then_inc(S(bb, "x"), 16)
                sy.wait_ge(S(bb, "ve"), bve + g * 3 + 3)
                sy.dma_start(
                    out=dst[g * TG:(g + 1) * TG, :].rearrange("(tt p) d -> p tt d", p=128),
                    in_=bb.tmp_sb[:, TT * D:2 * TT * D].rearrange("p (tt d) -> p tt d", tt=TT),
                ).then_inc(S(bb, "st"), 16)
            if final:
                sy.wait_ge(S(bb, "st"), bso + NTG * 16)

        @blk.vector
        def _(v):
            for g in range(NTG):
                v.wait_ge(S(bb, "x"), bx + g * 160 + 160)
                if g >= 1:
                    v.wait_ge(S(bb, "peo"), bst + (g - 1) * 2 + 2)
                for cc in range(DC):
                    v.tensor_mul(bb.hu_sb[:, cc * 256:(cc + 1) * 256],
                                 bb.uT_sb[:, cc * 256:(cc + 1) * 256],
                                 bb.uT_sb[:, 2048 + cc * 256:2048 + (cc + 1) * 256])
                v.tensor_copy(bb.misc[:, 126:127], bb.misc[:, 127:128])
                v.tensor_copy(bb.misc[:, 126:127],
                              bb.misc[:, 127:128]).then_inc(S(bb, "ve"), 1)
                for tt in range(TT):
                    v.wait_ge(S(bb, "peo"), bst + g * 2 + tt + 1)
                    if g >= 1 and tt == 0:
                        v.wait_ge(S(bb, "st"), bso + g * 16)
                    pso = bb.psC if tt == 0 else bb.psD
                    v.tensor_add(bb.tmp_sb[:, TT * D + tt * D:TT * D + (tt + 1) * D],
                                 pso[:, :],
                                 bb.x_sb[:, tt * D:(tt + 1) * D]).then_inc(S(bb, "ve"), 1)

        @blk.tensor
        def _(t):
            t.wait_ge(S(bb, "w"), bw + 16)
            for g in range(NTG):
                t.wait_ge(S(bb, "ve"), bve + g * 3 + 1)
                for tt in range(TT):
                    pso = bb.psC if tt == 0 else bb.psD
                    if g >= 1:
                        t.wait_ge(S(bb, "ve"), bve + (g - 1) * 3 + 1 + tt + 1)
                    for cc in range(DC):
                        pt = bb.hu_sb[:, cc * 256 + tt * 128: cc * 256 + (tt + 1) * 128]
                        t.ldweights(pt)
                        t.matmul(pso[:, 0:512], pt,
                                 bb.w2sb[:, cc * D:cc * D + 512],
                                 start=(cc == 0), stop=False)
                        t.matmul(pso[:, 512:1024], pt,
                                 bb.w2sb[:, cc * D + 512:(cc + 1) * D],
                                 start=(cc == 0), stop=False)
                    t.ldweights(bb.ones_sb[:, 0:128])
                    t.matmul(pso[:, 0:512], bb.ones_sb[:, 0:128],
                             bb.obr_sb[:, 0:512], start=False, stop=True)
                    t.matmul(pso[:, 512:1024], bb.ones_sb[:, 0:128],
                             bb.obr_sb[:, 512:1024], start=False,
                             stop=True).then_inc(S(bb, "peo"), 1)

    bb.cnt["w"] += 16
    bb.cnt["x"] += NTG * 160
    bb.cnt["ve"] += NTG * 3
    bb.cnt["peo"] += NTG * 2
    bb.cnt["st"] += NTG * 16


def emit_tap(bb, nc, src_flat, flat=True):
    waits = [(n, bb.cnt[n]) for n in ["acc", "u", "st", "fst", "x", "fh", "fa"]]
    with nc.Block() as blk:
        @blk.gpsimd
        def _(g):
            for n, cval in waits:
                g.wait_ge(S(bb, n), cval)
            g.dma_start(out=bb.y_sh.rearrange("t d -> (t d)"),
                        in_=src_flat).then_inc(S(bb, "st"), 16)
            g.wait_ge(S(bb, "st"), bb.cnt["st"] + 16)
    bb.cnt["st"] += 16



def build_nc():
    nc = bass.Bass("TRN2", target_bir_lowering=False, enable_partition_id=True)
    bb = BB()
    bb.ctx = ExitStack()
    _alloc(bb, nc)

    emit_p0(bb, nc)
    emit_lnpass(bb, nc, bb.x_sh, gating=True)
    if DEBUG_STAGE == 4:
        emit_tap(bb, nc, bb.u_scr.rearrange("d t -> (d t)"), flat=True)
        bb.ctx.close(); nc.finalize(); return nc
    if DEBUG_STAGE == 5:
        with nc.Block() as blk5:
            @blk5.sync
            def _(sy):
                sy.wait_ge(S(bb, "ve"), bb.cnt["ve"])
                sy.dma_start(out=bb.y_sh[0:128, 0:256], in_=bb.s_sb[:, :]
                             ).then_inc(S(bb, "st"), 16)
                sy.dma_start(out=bb.y_sh[128:256, 0:128], in_=bb.misc[:, :]
                             ).then_inc(S(bb, "st"), 16)
                sy.wait_ge(S(bb, "st"), bb.cnt["st"] + 32)
        bb.cnt["st"] += 32
        bb.ctx.close(); nc.finalize(); return nc
    for e in range(NE):
        emit_expert(bb, nc, e)
    if DEBUG_STAGE == 3:
        emit_tap(bb, nc, bb.y_scr.rearrange("t d -> (t d)"), flat=True)
        bb.ctx.close(); nc.finalize(); return nc
    if DEBUG_STAGE == 1:
        emit_postadd(bb, nc, bb.x_sh, bb.y_sh, final=True)
        bb.ctx.close()
        nc.finalize()
        return nc
    emit_postadd(bb, nc, bb.x_sh, bb.x1_scr, final=False)
    emit_lnpass(bb, nc, bb.x1_scr, gating=False)
    emit_projpass(bb, nc)
    if DEBUG_STAGE == 6:
        emit_tap(bb, nc, bb.vin_cc.rearrange("d t -> (d t)"))
        bb.ctx.close(); nc.finalize(); return nc
    if DEBUG_STAGE == 7:
        emit_tap(bb, nc, bb.xb2_scr.rearrange("d t -> (d t)"))
        bb.ctx.close(); nc.finalize(); return nc
    emit_hysetup(bb, nc)
    emit_ccv(bb, nc)
    if DEBUG_STAGE == 9:
        emit_tap(bb, nc, bb.filt_scr.rearrange("c l -> (c l)"))
        bb.ctx.close(); nc.finalize(); return nc
    if DEBUG_STAGE == 10:
        emit_tap(bb, nc, bb.hspec_scr[0:128, :].rearrange("c l -> (c l)"))
        bb.ctx.close(); nc.finalize(); return nc
    emit_shortconv(bb, nc)
    if DEBUG_STAGE == 11:
        emit_tap(bb, nc, bb.vpr_scr.rearrange("c l -> (c l)"))
        bb.ctx.close(); nc.finalize(); return nc
    emit_vfft(bb, nc)
    if DEBUG_STAGE == 8:
        emit_tap(bb, nc, bb.cin_cc.rearrange("a b -> (a b)"))
        bb.ctx.close(); nc.finalize(); return nc
    emit_ccconv(bb, nc)
    if DEBUG_STAGE == 2:
        emit_hyfinish(bb, nc, bb.y_sh, final=True)
        bb.ctx.close()
        nc.finalize()
        return nc
    emit_hyfinish(bb, nc, bb.x1_scr, final=False)
    emit_lnpass(bb, nc, bb.x1_scr, gating=True)
    for e in range(NE):
        emit_expert(bb, nc, e)
    emit_postadd(bb, nc, bb.x1_scr, bb.y_sh, final=True)
    bb.ctx.close()
    nc.finalize()
    return nc


def kernel(**inputs):
    shared, per_core = _prep(inputs)
    nc = build_nc()
    in_maps = []
    for c in range(8):
        m = dict(shared)
        m.update(per_core[c])
        in_maps.append(m)
    res = run_bass_kernel_spmd(nc, in_maps, core_ids=list(range(8)))
    out = np.zeros((4, 8192, 1024), dtype=np.float32)
    for c in range(8):
        b, h = divmod(c, 2)
        out[b, 4096 * h:4096 * h + 4096, :] = res.results[c]["y_sh"]
    return out
